# version 01
# baseline (speedup 1.0000x reference)
import contextlib

import ml_dtypes
import numpy as np

import concourse.bass as bass
import concourse.mybir as mybir
from concourse.bass_utils import run_bass_kernel_spmd

F32 = mybir.dt.float32
BF16 = mybir.dt.bfloat16
AF = mybir.ActivationFunctionType
ALU = mybir.AluOpType
AX = mybir.AxisListType
BFNP = ml_dtypes.bfloat16

NCORES = 8
D = 2048
H = 16
CH = 128
NCH = 8
NTOK = 1024
KD = 16
EPS = 1e-5


def _decay_rates():
    i = np.arange(H, dtype=np.float64)
    slopes = 2.0 ** (-(i + 1) / 2.0)
    return np.exp(-slopes)


def _host_constants():
    r = _decay_rates()
    c = np.arange(CH, dtype=np.float64)
    diff = c[None, :] - c[:, None]
    dm = np.where(diff >= 0, r[:, None, None] ** np.maximum(diff, 0.0)[None], 0.0)
    dmask = np.transpose(dm, (1, 0, 2)).reshape(CH, H * CH)
    qd = (r[:, None] ** (c[None, :] + 1.0))
    qdec = np.broadcast_to(qd.reshape(1, H * CH), (CH, H * CH))
    rci = np.zeros((CH, H * CH))
    for h in range(H):
        rci[:, h * CH:(h + 1) * CH] = np.eye(CH) * (r[h] ** CH)
    kdec = (r[None, :] ** (CH - 1.0 - c[:, None]))
    sw = np.zeros((CH, CH))
    for k in range(CH):
        sw[k, (k + 64) % CH] = 1.0
    return {
        "dmask": dmask.astype(BFNP),
        "qdec": np.ascontiguousarray(qdec).astype(BFNP),
        "rci": rci.astype(BFNP),
        "kdecc": kdec.astype(np.float32),
        "swapm": sw.astype(BFNP),
        "identb": np.eye(CH).astype(BFNP),
        "identf": np.eye(8).astype(np.float32),
    }


def _host_core_constants(core):
    r = _decay_rates()
    t = np.arange(NTOK, dtype=np.float64) + core * NTOK
    i = np.arange(32, dtype=np.float64)
    freq = (1.0 / 1024.0) ** (i / 31.0)
    theta = t[None, :] * freq[:, None]
    cosr = np.ones((CH, NTOK))
    sinr = np.zeros((CH, NTOK))
    cosr[0:32] = np.cos(theta)
    cosr[64:96] = np.cos(theta)
    sinr[0:32] = np.sin(theta)
    sinr[64:96] = -np.sin(theta)
    dec8 = r ** (CH * NCH)
    coef = np.zeros((CH, CH))
    for ii in range(NCORES):
        if ii < core:
            coef[:, 16 * ii:16 * ii + 16] = dec8[None, :] ** (core - 1 - ii)
    return {
        "cosr": cosr.astype(BFNP),
        "sinr": sinr.astype(BFNP),
        "coefc": coef.astype(np.float32),
    }


def _host_weights(Wq, Wk, Wv, Wo, rms_w, A, Bw, Wg):
    def m_major(W):
        return np.ascontiguousarray(
            W.reshape(16, 128, 16, 128).transpose(2, 1, 0, 3)).reshape(16, 128, 2048)
    wv_r = np.ascontiguousarray(
        Wv.reshape(16, 128, 4, 512).transpose(2, 1, 0, 3)).reshape(4, 128, 8192)
    a_r = np.ascontiguousarray(
        A.transpose(2, 0, 1).reshape(2048, 512).reshape(16, 128, 512)
        .transpose(1, 0, 2)).reshape(128, 8192)
    bwt = Bw.transpose(0, 2, 1).reshape(512, 2048)
    bw_r = np.ascontiguousarray(
        bwt.reshape(4, 128, 16, 128).transpose(2, 1, 0, 3)).reshape(16, 128, 512)
    wg_r = np.ascontiguousarray(
        Wg.T.reshape(16, 128, 8).transpose(1, 0, 2)).reshape(128, 128)
    rms_r = np.ascontiguousarray(rms_w.reshape(16, 128).T)
    return {
        "wq": m_major(Wq), "wk": m_major(Wk), "wo": m_major(Wo),
        "wv": wv_r, "aall": a_r, "bwm": bw_r, "wg": wg_r, "rmsw": rms_r,
    }


class Sem:
    def __init__(self, h):
        self.h = h
        self.n = 0

    def bump(self, k=1):
        self.n += k
        return self.n


def build_nc(taps=()):
    nc = bass.Bass("TRN2", target_bir_lowering=False,
                   dynamic_dma_scratch_size=12288)

    def din(name, shape, dt):
        return nc.dram_tensor(name, shape, dt, kind="ExternalInput")

    xt = din("xt", [D, NTOK], F32)
    wq = din("wq", [16, 128, 2048], F32)
    wk = din("wk", [16, 128, 2048], F32)
    wo = din("wo", [16, 128, 2048], F32)
    wv = din("wv", [4, 128, 8192], F32)
    aall = din("aall", [128, 8192], F32)
    bwm = din("bwm", [16, 128, 512], F32)
    wg = din("wg", [128, 128], F32)
    rmsw = din("rmsw", [128, 16], F32)
    dmaskd = din("dmask", [128, 2048], BF16)
    qdecd = din("qdec", [128, 2048], BF16)
    rcid = din("rci", [128, 2048], BF16)
    kdeccd = din("kdecc", [128, 16], F32)
    coefcd = din("coefc", [128, 128], F32)
    cosrd = din("cosr", [128, 1024], BF16)
    sinrd = din("sinr", [128, 1024], BF16)
    swapmd = din("swapm", [128, 128], BF16)
    identbd = din("identb", [128, 128], BF16)
    identfd = din("identf", [8, 8], F32)
    out_t = nc.dram_tensor("out", [D, NTOK], F32, kind="ExternalOutput")
    cc_in = nc.dram_tensor("cc_in", [128, 2048], F32)
    cc_out = nc.dram_tensor("cc_out", [1024, 2048], F32)
    dbg = {}
    for tp in taps:
        shp, dt = taps[tp]
        dbg[tp] = nc.dram_tensor("dbg_" + tp, shp, dt, kind="ExternalOutput")

    def sb(name, shape, dt):
        return nc.alloc_sbuf_tensor(name, shape, dt)

    BUF_A = sb("buf_a", [128, 16384], F32)
    BUF_B = sb("buf_b", [128, 8192], F32)
    BUF_C = sb("buf_c", [128, 8192], F32)
    HT = sb("ht", [128, 4096], BF16)
    VPX = sb("vpx", [128, 4096], F32)
    SCR = sb("scr", [128, 4096], F32)
    WSL = sb("wsl", [128, 4096], BF16)
    BWS = sb("bws", [128, 1024], BF16)
    QD = sb("qd", [128, 2048], BF16)
    SCM = sb("scm", [128, 1024], BF16)
    STBF = sb("stbf", [128, 2048], BF16)
    DMASKC = sb("dmaskc", [128, 2048], BF16)
    QDECC = sb("qdecc", [128, 2048], BF16)
    RCIC = sb("rcic", [128, 2048], BF16)
    COSR = sb("cosrc", [128, 1024], BF16)
    SINR = sb("sinrc", [128, 1024], BF16)
    KDECC = sb("kdeccc", [128, 16], F32)
    RMSC = sb("rmsc", [128, 16], F32)
    COEFC = sb("coefcc", [128, 128], F32)
    WGT = sb("wgt", [128, 128], BF16)
    IDENTB = sb("identbc", [128, 128], BF16)
    IDENTF = sb("identfc", [8, 8], F32)
    SWAPM = sb("swapmc", [128, 128], BF16)
    ONESC = sb("onesc", [128, 1], BF16)
    EPSC = sb("epsc", [128, 1], F32)
    GW = sb("gw", [128, 432], F32)
    GEXP = sb("gexp", [128, 8], F32)

    PS_A = nc.alloc_psum_tensor("ps_a", [128, 2048], F32)
    PS_B = nc.alloc_psum_tensor("ps_b", [128, 1024], F32)
    PS_C = nc.alloc_psum_tensor("ps_c", [128, 1024], F32)

    bcA = BUF_A.ap().bitcast(BF16).tensor
    bcB = BUF_B.ap().bitcast(BF16).tensor
    bcC = BUF_C.ap().bitcast(BF16).tensor
    bcV = VPX.ap().bitcast(BF16).tensor
    bcS = SCR.ap().bitcast(BF16).tensor
    bcPB = PS_B.ap().bitcast(BF16).tensor
    bcPC = PS_C.ap().bitcast(BF16).tensor
    KT0 = 16384
    bcSTBF = STBF.ap().bitcast(F32).tensor
    bcSCM = SCM.ap().bitcast(F32).tensor

    def ap(t, col, dims, prow=0):
        pitch = t.shape[-1]
        return bass.AP(t, prow * pitch + col, dims)

    ST = {"pe": [], "dv": [], "ac": [], "po": [], "sy": []}

    def em(k, f):
        ST[k].append(f)

    with contextlib.ExitStack() as es:
        s_pe = Sem(es.enter_context(nc.semaphore("s_pe")))
        s_dv = Sem(es.enter_context(nc.semaphore("s_dv")))
        s_ac = Sem(es.enter_context(nc.semaphore("s_ac")))
        s_po = Sem(es.enter_context(nc.semaphore("s_po")))
        s_cc = Sem(es.enter_context(nc.semaphore("s_cc")))
        dma_p = Sem(es.enter_context(nc.semaphore("dma_p")))
        dma_s = Sem(es.enter_context(nc.semaphore("dma_s")))
        ALLSEMS = [s_pe, s_dv, s_ac, s_po, s_cc, dma_p, dma_s]
        MYSEM = {"pe": s_pe, "dv": s_dv, "ac": s_ac, "po": s_po, "sy": None}

        def wt(k, sem, v):
            if v and v > 0:
                em(k, lambda e, s=sem.h, vv=v: e.wait_ge(s, vv))

        def op(k, name, *a, inc=None, **kw):
            v = inc.bump(1) if inc is not None else None
            def f(e, name=name, a=a, kw=kw, inc=inc):
                ins = getattr(e, name)(*a, **kw)
                if inc is not None:
                    ins.then_inc(inc.h, 1)
            em(k, f)
            return v

        def dma(k, sem, out_ap, in_ap):
            v = sem.bump(16)
            em(k, lambda e, o=out_ap, i=in_ap, s=sem.h:
               e.dma_start(out=o, in_=i).then_inc(s, 16))
            return v

        def barrier():
            snap = [(s, s.n) for s in ALLSEMS]
            for k in ST:
                own = MYSEM[k]
                for s, v in snap:
                    if s is not own and v > 0:
                        wt(k, s, v)

        def tap(name, src_ap):
            if name in dbg:
                dma("sy", dma_s, dbg[name].ap(), src_ap)

        MM = dict(skip_group_check=True)

        x_in = ap(xt, 0, [[1024, 128], [131072, 16], [1, 1024]])
        x_out = ap(BUF_A, 0, [[16384, 128], [1024, 16], [1, 1024]])
        dma("sy", dma_s, x_out, x_in)
        for dst, src in [(DMASKC, dmaskd), (QDECC, qdecd), (RCIC, rcid),
                         (COSR, cosrd), (SINR, sinrd), (KDECC, kdeccd),
                         (RMSC, rmsw), (COEFC, coefcd), (IDENTB, identbd),
                         (IDENTF, identfd), (SWAPM, swapmd)]:
            dma("sy", dma_s, dst.ap(), src.ap())
        xb_out = ap(bcB, 0, [[16384, 128], [1024, 16], [1, 1024]])
        dma("po", dma_p, xb_out, x_in)
        dma("po", dma_p, WGT.ap(), wg.ap())
        dma("po", dma_p, ap(bcV, 0, [[8192, 128], [1, 8192]]), aall.ap())
        op("po", "memset", ONESC.ap(), 1.0)
        op("po", "memset", GW.ap()[0:1, 304:432], 1.0)
        op("po", "memset", EPSC.ap(), EPS, inc=s_po)
        v_sy0, v_po0, v_ms0 = dma_s.n, dma_p.n, s_po.n
        for k in ("pe", "dv", "ac", "sy"):
            wt(k, dma_s, v_sy0)
            wt(k, dma_p, v_po0)
            wt(k, s_po, v_ms0)

        for kd in range(KD):
            for hf in range(2):
                op("pe", "matmul", PS_B.ap()[0:1, 512 * hf:512 * hf + 512],
                   ONESC.ap(), ap(bcB, 1024 * kd + 512 * hf, [[16384, 128], [1, 512]]),
                   start=(kd == 0), stop=(kd == KD - 1),
                   inc=(s_pe if kd == KD - 1 and hf == 1 else None), **MM)
        v_sq1, v_sqmm1 = [], []
        for kd in range(KD):
            if kd >= 2:
                wt("ac", s_pe, v_sqmm1[kd - 2])
            v_sq1.append(op("ac", "activation",
                            QD.ap()[:, 1024 * (kd % 2):1024 * (kd % 2) + 1024],
                            BUF_A.ap()[:, 1024 * kd:1024 * kd + 1024],
                            AF.Square, inc=s_ac))
            wt("pe", s_ac, v_sq1[kd])
            for hf in range(2):
                op("pe", "matmul", PS_C.ap()[0:1, 512 * hf:512 * hf + 512],
                   ONESC.ap(),
                   QD.ap()[:, 1024 * (kd % 2) + 512 * hf:1024 * (kd % 2) + 512 * hf + 512],
                   start=(kd == 0), stop=(kd == KD - 1),
                   inc=(s_pe if hf == 1 else None), **MM)
            v_sqmm1.append(s_pe.n)
        v_stats = v_sqmm1[-1]

        row_mean = SCR.ap()[0:1, 1024:2048]
        row_rstd = SCR.ap()[0:1, 2048:3072]
        row_mrs = SCR.ap()[0:1, 3072:4096]
        row_tmp = SCR.ap()[0:1, 0:1024]
        wt("dv", s_pe, v_stats)
        op("dv", "tensor_scalar_mul", row_mean, PS_B.ap()[0:1, 0:1024], 1.0 / D)
        op("dv", "drain")
        op("dv", "tensor_mul", row_tmp, row_mean, row_mean)
        op("dv", "tensor_scalar_mul", row_rstd, PS_C.ap()[0:1, 0:1024], 1.0 / D)
        op("dv", "drain")
        v_var = op("dv", "tensor_sub", row_rstd, row_rstd, row_tmp, inc=s_dv)
        wt("ac", s_dv, v_var)
        op("ac", "activation", row_tmp, row_rstd, AF.Ln, bias=EPSC.ap()[0:1, :], scale=1.0)
        v_rstd = op("ac", "activation", row_rstd, row_tmp, AF.Exp, scale=-0.5, inc=s_ac)
        wt("dv", s_ac, v_rstd)
        v_mrs = op("dv", "tensor_mul", row_mrs, row_mean, row_rstd, inc=s_dv)
        wt("pe", s_dv, v_mrs)
        for hf in range(2):
            op("pe", "matmul", PS_B.ap()[:, 512 * hf:512 * hf + 512], GW.ap()[0:1, 304:432],
               SCR.ap()[0:1, 2048 + 512 * hf:2048 + 512 * hf + 512],
               start=True, stop=True, **MM)
            op("pe", "matmul", PS_C.ap()[:, 512 * hf:512 * hf + 512], GW.ap()[0:1, 304:432],
               SCR.ap()[0:1, 3072 + 512 * hf:3072 + 512 * hf + 512],
               start=True, stop=True, inc=(s_pe if hf == 1 else None), **MM)
        v_bcast = s_pe.n

        xnf = SCR.ap()[:, 0:1024]
        v_xn = []
        wt("dv", s_pe, v_bcast)
        for kd in range(KD):
            op("dv", "tensor_mul", xnf, BUF_A.ap()[:, 1024 * kd:1024 * kd + 1024],
               PS_B.ap()[:, 0:1024])
            v_xn.append(op("dv", "tensor_sub",
                           ap(bcC, 1024 * kd, [[16384, 128], [1, 1024]]),
                           xnf, PS_C.ap()[:, 0:1024], inc=s_dv))
        wt("pe", s_dv, v_xn[-1])
        for c in range(NCH):
            for kd in range(KD):
                op("pe", "matmul", PS_C.ap()[:, 8 * c:8 * c + 8],
                   ap(bcC, 1024 * kd + 128 * c, [[16384, 128], [1, 128]]),
                   WGT.ap()[:, 8 * kd:8 * kd + 8],
                   start=(kd == 0), stop=(kd == KD - 1),
                   inc=(s_pe if c == NCH - 1 and kd == KD - 1 else None), **MM)
        v_logitsmm = s_pe.n
        if "xn" in dbg:
            wt("sy", s_dv, v_xn[-1])
            tap("xn", ap(bcC, 0, [[16384, 128], [1, 16384]]))

        wt("dv", s_pe, v_logitsmm)
        op("dv", "tensor_copy", ap(bcSCM, 0, [[512, 128], [1, 64]]), PS_C.ap()[:, 0:64])
        for c in range(NCH):
            lg = ap(bcSCM, 8 * c, [[512, 128], [1, 8]])
            op("dv", "reduce_max", GW.ap()[:, 0:1], lg, AX.X)
            op("dv", "drain")
            v_g1 = op("dv", "tensor_scalar_mul", GW.ap()[:, 1:2], GW.ap()[:, 0:1],
                      -1.0, inc=s_dv)
            wt("ac", s_dv, v_g1)
            v_g2c = op("ac", "activation", GEXP.ap(), lg, AF.Exp,
                       bias=GW.ap()[:, 1:2], scale=1.0, inc=s_ac)
            wt("dv", s_ac, v_g2c)
            op("dv", "reduce_sum", GW.ap()[:, 2:3], GEXP.ap(), AX.X)
            op("dv", "drain")
            op("dv", "reciprocal", GW.ap()[:, 3:4], GW.ap()[:, 2:3])
            op("dv", "drain")
            op("dv", "tensor_scalar_mul", GW.ap()[:, 2:3], GW.ap()[:, 3:4], 4.0)
            op("dv", "tensor_single_scalar", GEXP.ap(), lg, GW.ap()[:, 0:1], ALU.is_equal)
            op("dv", "drain")
            op("dv", "tensor_scalar_mul", ap(bcSTBF, 8 * c, [[1024, 128], [1, 8]]), GEXP.ap(),
               GW.ap()[:, 2:3], inc=(s_dv if c == NCH - 1 else None))
        v_sgate = s_dv.n
        if "sgate" in dbg:
            wt("sy", s_dv, v_sgate)
            tap("sgate", ap(bcSTBF, 0, [[1024, 128], [1, 64]]))
            wt("ac", dma_s, dma_s.n)
        if "lgtm" in dbg:
            wt("sy", s_dv, v_sgate)
            tap("lgtm", ap(bcSCM, 0, [[512, 128], [1, 64]]))
            wt("dv", dma_s, dma_s.n)

        v_hmm, v_hevac, v_hTmm, v_hTevac = [], [], [], []
        for c in range(NCH):
            hp = PS_A.ap()[:, 1024 * (c % 2):1024 * (c % 2) + 512]
            if c >= 2:
                wt("pe", s_dv, v_hevac[c - 2])
            for k in range(KD):
                op("pe", "matmul", hp,
                   ap(bcC, 1024 * k + 128 * c, [[16384, 128], [1, 128]]),
                   ap(bcV, 512 * k, [[8192, 128], [1, 512]]),
                   start=(k == 0), stop=(k == KD - 1),
                   inc=(s_pe if k == KD - 1 else None), **MM)
            v_hmm.append(s_pe.n)
            wt("dv", s_pe, v_hmm[c])
            if c >= 1:
                wt("dv", s_pe, v_hTmm[c - 1])
            for ee in range(8):
                op("dv", "tensor_scalar_mul", SCM.ap()[:, 64 * ee:64 * ee + 64],
                   bass.AP(PS_A, 1024 * (c % 2) + 64 * ee, [[2048, 128], [1, 64]]),
                   ap(bcSTBF, 8 * c + ee, [[1024, 128], [1, 1]]),
                   inc=(s_dv if ee == 7 else None))
            v_hevac.append(s_dv.n)
            wt("pe", s_dv, v_hevac[c])
            if c == 0:
                wt("pe", s_dv, v_sgate)
            if c >= 1:
                wt("pe", s_dv, v_hTevac[c - 1])
            for b in range(4):
                op("pe", "transpose", ap(bcPC, 128 * b, [[2048, 128], [1, 128]]),
                   SCM.ap()[:, 128 * b:128 * b + 128], IDENTB.ap(),
                   inc=(s_pe if b == 3 else None))
            v_hTmm.append(s_pe.n)
            wt("dv", s_pe, v_hTmm[c])
            for b in range(4):
                op("dv", "tensor_copy",
                   HT.ap()[:, 1024 * b + 128 * c:1024 * b + 128 * c + 128],
                   ap(bcPC, 128 * b, [[2048, 128], [1, 128]]),
                   inc=(s_dv if b == 3 else None))
            v_hTevac.append(s_dv.n)
        if "hT" in dbg:
            wt("sy", s_dv, v_hTevac[-1])
            tap("hT", HT.ap())

        barrier()

        def projection(w_t, colbase):
            v_wsl, v_pmm, v_pevac, v_swpmm, v_swpevac, v_rot = {}, {}, {}, {}, {}, {}

            def swap_psum(m):
                return PS_B if (m % 2 == 0) else PS_C

            def emit_swap_mm(m):
                wt("pe", s_ac, v_pevac[m])
                if m >= 2:
                    wt("pe", s_ac, v_swpevac[m - 2])
                sp = swap_psum(m)
                for hf in range(2):
                    op("pe", "matmul", sp.ap()[:, 512 * hf:512 * hf + 512],
                       SWAPM.ap(),
                       ap(bcA, colbase + 1024 * m + 512 * hf, [[32768, 128], [1, 512]]),
                       start=True, stop=True, inc=(s_pe if hf == 1 else None), **MM)
                v_swpmm[m] = s_pe.n

            def emit_swp_evac(m):
                wt("ac", s_pe, v_swpmm[m])
                if m >= 2:
                    wt("ac", s_dv, v_rot[m - 2])
                v_swpevac[m] = op("ac", "copy",
                                  ap(bcV, 1024 * (m % 2), [[8192, 128], [1, 1024]]),
                                  swap_psum(m).ap()[:, 0:1024], inc=s_ac)

            def emit_rot(m):
                wt("dv", s_ac, v_swpevac[m])
                tile = ap(bcA, colbase + 1024 * m, [[32768, 128], [1, 1024]])
                rott = ap(bcS, 0, [[8192, 128], [1, 1024]])
                rotu = ap(bcS, 1024, [[8192, 128], [1, 1024]])
                op("dv", "tensor_mul", rott, tile, COSR.ap())
                op("dv", "tensor_mul", rotu,
                   ap(bcV, 1024 * (m % 2), [[8192, 128], [1, 1024]]), SINR.ap())
                v_rot[m] = op("dv", "tensor_add", tile, rott, rotu, inc=s_dv)

            for m in range(16):
                if m >= 2:
                    wt("po", s_pe, v_pmm[m - 2])
                v_wsl[m] = dma("po", dma_p,
                               WSL.ap()[:, 2048 * (m % 2):2048 * (m % 2) + 2048],
                               ap(w_t, m * 128 * 2048, [[2048, 128], [1, 2048]]))
                wt("pe", dma_p, v_wsl[m])
                if m >= 2:
                    wt("pe", s_ac, v_pevac[m - 2])
                for k in range(KD):
                    for hf in range(2):
                        op("pe", "matmul",
                           PS_A.ap()[:, 1024 * (m % 2) + 512 * hf:
                                     1024 * (m % 2) + 512 * hf + 512],
                           WSL.ap()[:, 2048 * (m % 2) + 128 * k:
                                    2048 * (m % 2) + 128 * k + 128],
                           ap(bcC, 1024 * k + 512 * hf, [[16384, 128], [1, 512]]),
                           start=(k == 0), stop=(k == KD - 1),
                           inc=(s_pe if k == KD - 1 and hf == 1 else None), **MM)
                v_pmm[m] = s_pe.n
                if m >= 1:
                    emit_swap_mm(m - 1)
                wt("ac", s_pe, v_pmm[m])
                v_pevac[m] = op("ac", "copy",
                                ap(bcA, colbase + 1024 * m, [[32768, 128], [1, 1024]]),
                                PS_A.ap()[:, 1024 * (m % 2):1024 * (m % 2) + 1024],
                                inc=s_ac)
                if m >= 1:
                    emit_swp_evac(m - 1)
                    emit_rot(m - 1)
            emit_swap_mm(15)
            emit_swp_evac(15)
            emit_rot(15)

        projection(wk, KT0)
        barrier()

        def v_psum(c):
            if c < 4:
                return bass.AP(PS_A, 512 * c, [[2048, 128], [1, 512]])
            if c < 6:
                return PS_B.ap()[:, 512 * (c - 4):512 * (c - 4) + 512]
            return PS_C.ap()[:, 512 * (c - 6):512 * (c - 6) + 512]

        v_vp, v_vhalf, v_vcmm, v_vevac = {}, {}, {}, {}
        for g in range(4):
            for hf in range(2):
                if g >= 1:
                    wt("po", s_pe, v_vhalf[(g - 1, hf)])
                v_vp[(g, hf)] = dma("po", dma_p,
                                    ap(bcV, 4096 * hf, [[8192, 128], [1, 4096]]),
                                    ap(wv, g * 128 * 8192 + 4096 * hf,
                                       [[8192, 128], [1, 4096]]))
            for hf in range(2):
                wt("pe", dma_p, v_vp[(g, hf)])
                if g >= 1 and hf == 0:
                    wt("pe", s_ac, v_vevac[(g - 1, 7)])
                for c in range(NCH):
                    for kk in range(8):
                        k = 8 * hf + kk
                        op("pe", "matmul", v_psum(c),
                           ap(bcC, 1024 * k + 128 * c, [[16384, 128], [1, 128]]),
                           ap(bcV, 4096 * hf + 512 * kk, [[8192, 128], [1, 512]]),
                           start=(k == 0), stop=(k == KD - 1),
                           inc=(s_pe if kk == 7 else None), **MM)
                    if hf == 1:
                        v_vcmm[(g, c)] = s_pe.n
                v_vhalf[(g, hf)] = s_pe.n
            for c in range(NCH):
                wt("ac", s_pe, v_vcmm[(g, c)])
                v_vevac[(g, c)] = op("ac", "copy",
                                     ap(bcB, 2048 * c + 512 * g, [[16384, 128], [1, 512]]),
                                     v_psum(c), inc=s_ac)
        barrier()
        if "v" in dbg:
            tap("v", ap(bcB, 0, [[16384, 128], [1, 16384]]))

        projection(wq, 0)
        barrier()
        if "qT" in dbg:
            tap("qT", ap(bcA, 0, [[32768, 128], [1, 16384]]))
        if "kT" in dbg:
            tap("kT", ap(bcA, KT0, [[32768, 128], [1, 16384]]))

        def kd_psum_ap(par, slot):
            return ap(bcPB, 1024 * par + 128 * slot, [[2048, 128], [1, 128]])

        v_kdT, v_kdevac = {}, {}
        v_p1state, v_stbf1 = {}, {}
        for c in range(NCH):
            if c > 0:
                wt("ac", s_pe, v_p1state[c - 1])
                v_stbf1[c] = op("ac", "copy", STBF.ap(), PS_A.ap()[:, 0:2048], inc=s_ac)

            def emit_kdT(G, c=c):
                gn = 4 * c + G
                if gn >= 2:
                    wt("pe", s_dv, v_kdevac[gn - 2])
                for h in range(4 * G, 4 * G + 4):
                    op("pe", "transpose", kd_psum_ap(gn % 2, h % 4),
                       ap(bcA, KT0 + 1024 * h + 128 * c, [[32768, 128], [1, 128]]),
                       IDENTB.ap(), inc=(s_pe if h % 4 == 3 else None))
                v_kdT[gn] = s_pe.n

            def emit_state(G, c=c):
                gn = 4 * c + G
                wt("pe", s_dv, v_kdevac[gn])
                if G == 0 and c > 0:
                    wt("pe", s_ac, v_stbf1[c])
                for h in range(4 * G, 4 * G + 4):
                    kd_sl = ap(bcC, 2048 * c + 128 * h, [[16384, 128], [1, 128]])
                    v_sl = ap(bcB, 2048 * c + 128 * h, [[16384, 128], [1, 128]])
                    st_out = PS_A.ap()[:, 128 * h:128 * h + 128]
                    last = (s_pe if h % 4 == 3 else None)
                    if c == 0:
                        op("pe", "matmul", st_out, kd_sl, v_sl, start=True, stop=True,
                           inc=last, **MM)
                    else:
                        op("pe", "matmul", st_out, RCIC.ap()[:, 128 * h:128 * h + 128],
                           STBF.ap()[:, 128 * h:128 * h + 128], start=True, stop=False,
                           **MM)
                        op("pe", "matmul", st_out, kd_sl, v_sl, start=False, stop=True,
                           inc=last, **MM)
                if G == 3:
                    v_p1state[c] = s_pe.n

            def emit_kdevac(G, c=c):
                gn = 4 * c + G
                wt("dv", s_pe, v_kdT[gn])
                for h in range(4 * G, 4 * G + 4):
                    op("dv", "tensor_scalar_mul",
                       ap(bcC, 2048 * c + 128 * h, [[16384, 128], [1, 128]]),
                       kd_psum_ap(gn % 2, h % 4), KDECC.ap()[:, h:h + 1],
                       inc=(s_dv if h % 4 == 3 else None))
                v_kdevac[gn] = s_dv.n

            emit_kdT(0)
            emit_kdevac(0)
            emit_kdT(1)
            emit_state(0)
            emit_kdevac(1)
            emit_kdT(2)
            emit_state(1)
            emit_kdevac(2)
            emit_kdT(3)
            emit_state(2)
            emit_kdevac(3)
            emit_state(3)

        wt("ac", s_pe, v_p1state[NCH - 1])
        v_Sevac = op("ac", "copy", VPX.ap()[:, 0:2048], PS_A.ap()[:, 0:2048], inc=s_ac)
        wt("sy", s_ac, v_Sevac)
        v_ccin = dma("sy", dma_s, cc_in.ap(), VPX.ap()[:, 0:2048])
        if "S" in dbg:
            tap("S", VPX.ap()[:, 0:2048])
        wt("po", dma_s, v_ccin)
        v_ccc = s_cc.bump()
        em("po", lambda e, s=s_cc.h: e.collective_compute(
            "AllGather", ALU.bypass, replica_groups=[list(range(NCORES))],
            ins=[cc_in.ap().opt()], outs=[cc_out.ap().opt()]).then_inc(s, 1))

        v_wslw, v_bwsw = {}, {}
        for m in range(2):
            v_wslw[m] = dma("po", dma_p,
                            WSL.ap()[:, 2048 * m:2048 * m + 2048],
                            ap(wo, m * 128 * 2048, [[2048, 128], [1, 2048]]))
            v_bwsw[m] = dma("po", dma_p,
                            BWS.ap()[:, 512 * m:512 * m + 512],
                            ap(bwm, m * 128 * 512, [[512, 128], [1, 512]]))

        v_gth, v_comb = {}, {}
        wt("sy", s_cc, v_ccc)
        for i in range(NCORES):
            if i >= 2:
                wt("sy", s_dv, v_comb.get(i - 2, 0))
            v_gth[i] = dma("sy", dma_s,
                           VPX.ap()[:, 2048 * (i % 2):2048 * (i % 2) + 2048],
                           ap(cc_out, i * 128 * 2048, [[2048, 128], [1, 2048]]))
            wt("dv", dma_s, v_gth[i])
            for h in range(H):
                gsl = VPX.ap()[:, 2048 * (i % 2) + 128 * h:2048 * (i % 2) + 128 * h + 128]
                ksl = SCR.ap()[:, 128 * h:128 * h + 128]
                co = COEFC.ap()[:, 16 * i + h:16 * i + h + 1]
                last = (s_dv if h == H - 1 else None)
                if i == 0:
                    op("dv", "tensor_scalar_mul", ksl, gsl, co, inc=last)
                else:
                    op("dv", "scalar_tensor_tensor", ksl, gsl, co, ksl,
                       ALU.mult, ALU.add, inc=last)
            v_comb[i] = s_dv.n
        wt("ac", s_dv, v_comb[NCORES - 1])
        kvin_bf = ap(bcS, 4096, [[8192, 128], [1, 2048]])
        v_kvinbf = op("ac", "copy", kvin_bf, SCR.ap()[:, 0:2048], inc=s_ac)
        if "KVin" in dbg:
            wt("sy", s_dv, v_comb[NCORES - 1])
            tap("KVin", SCR.ap()[:, 0:2048])
        barrier()

        attn_sc = SCR.ap()[:, 0:2048]
        sqv = ap(bcS, 6144, [[8192, 128], [1, 2048]])
        v_qd, v_stbf2, v_scores, v_maskevac, v_outmm = {}, {}, {}, {}, {}
        v_outevac, v_p2state, v_sqa, v_sqmm, v_rmsrow = {}, {}, {}, {}, {}
        v_rstdbcmm, v_rstdsb, v_rms = {}, {}, {}
        for c in range(NCH):
            if c > 0:
                wt("dv", s_pe, v_p2state[c - 1])
            v_qd[c] = op("dv", "tensor_mul",
                         ap(QD, 0, [[2048, 128], [128, 16], [1, 128]]),
                         ap(bcA, 128 * c, [[32768, 128], [1024, 16], [1, 128]]),
                         ap(QDECC, 0, [[2048, 128], [128, 16], [1, 128]]),
                         inc=s_dv)
            if c > 0:
                wt("ac", s_pe, v_p2state[c - 1])
                v_stbf2[c] = op("ac", "copy", STBF.ap(), PS_A.ap()[:, 0:2048], inc=s_ac)

            def emit_scores(G, c=c):
                gn = 4 * c + G
                if gn >= 2:
                    wt("pe", s_dv, v_maskevac[gn - 2])
                for h in range(4 * G, 4 * G + 4):
                    sl = PS_B.ap()[:, 512 * (gn % 2) + 128 * (h % 4):
                                   512 * (gn % 2) + 128 * (h % 4) + 128]
                    op("pe", "matmul", sl,
                       ap(bcA, KT0 + 1024 * h + 128 * c, [[32768, 128], [1, 128]]),
                       ap(bcA, 1024 * h + 128 * c, [[32768, 128], [1, 128]]),
                       start=True, stop=True, inc=(s_pe if h % 4 == 3 else None), **MM)
                v_scores[gn] = s_pe.n

            def emit_mask(G, c=c):
                gn = 4 * c + G
                wt("dv", s_pe, v_scores[gn])
                prevkey = (c, G - 2) if G >= 2 else (c - 1, G + 2)
                if prevkey in v_outmm:
                    wt("dv", s_pe, v_outmm[prevkey])
                op("dv", "tensor_mul", SCM.ap()[:, 512 * (gn % 2):512 * (gn % 2) + 512],
                   PS_B.ap()[:, 512 * (gn % 2):512 * (gn % 2) + 512],
                   DMASKC.ap()[:, 512 * G:512 * G + 512], inc=s_dv)
                v_maskevac[gn] = s_dv.n

            def emit_out(G, c=c):
                gn = 4 * c + G
                wt("pe", s_dv, v_maskevac[gn])
                if G == 0:
                    wt("pe", s_dv, v_qd[c])
                    if c > 0:
                        wt("pe", s_ac, v_stbf2[c])
                        wt("pe", s_ac, v_rmsrow[c - 1])
                    else:
                        wt("pe", s_ac, v_kvinbf)
                if G == 1 and c > 0:
                    wt("pe", s_ac, v_rstdsb[c - 1])
                prevkey = (c, G - 2) if G >= 2 else (c - 1, G + 2)
                if prevkey in v_outevac:
                    wt("pe", s_ac, v_outevac[prevkey])
                for h in range(4 * G, 4 * G + 4):
                    sl = PS_C.ap()[:, 512 * (gn % 2) + 128 * (h % 4):
                                   512 * (gn % 2) + 128 * (h % 4) + 128]
                    op("pe", "matmul", sl,
                       ap(bcB, 2048 * c + 128 * h, [[16384, 128], [1, 128]]),
                       SCM.ap()[:, 512 * (gn % 2) + 128 * (h % 4):
                                512 * (gn % 2) + 128 * (h % 4) + 128],
                       start=True, stop=False, **MM)
                    if c == 0:
                        stsl = ap(bcS, 4096 + 128 * h, [[8192, 128], [1, 128]])
                    else:
                        stsl = STBF.ap()[:, 128 * h:128 * h + 128]
                    op("pe", "matmul", sl, stsl, QD.ap()[:, 128 * h:128 * h + 128],
                       start=False, stop=True, inc=(s_pe if h % 4 == 3 else None), **MM)
                v_outmm[(c, G)] = s_pe.n

            def emit_state2(G, c=c):
                if c == NCH - 1:
                    return
                for h in range(4 * G, 4 * G + 4):
                    st_out = PS_A.ap()[:, 128 * h:128 * h + 128]
                    if c == 0:
                        ssl = ap(bcS, 4096 + 128 * h, [[8192, 128], [1, 128]])
                    else:
                        ssl = STBF.ap()[:, 128 * h:128 * h + 128]
                    op("pe", "matmul", st_out, RCIC.ap()[:, 128 * h:128 * h + 128],
                       ssl, start=True, stop=False, **MM)
                    op("pe", "matmul", st_out,
                       ap(bcC, 2048 * c + 128 * h, [[16384, 128], [1, 128]]),
                       ap(bcB, 2048 * c + 128 * h, [[16384, 128], [1, 128]]),
                       start=False, stop=True,
                       inc=(s_pe if G == 3 and h % 4 == 3 else None), **MM)
                if G == 3:
                    v_p2state[c] = s_pe.n

            def emit_outevac(G, c=c):
                gn = 4 * c + G
                wt("ac", s_pe, v_outmm[(c, G)])
                if G == 0 and c > 0:
                    wt("ac", s_dv, v_rms[c - 1])
                v_outevac[(c, G)] = op("ac", "copy",
                                       SCR.ap()[:, 512 * G:512 * G + 512],
                                       PS_C.ap()[:, 512 * (gn % 2):512 * (gn % 2) + 512],
                                       inc=s_ac)

            emit_scores(0)
            emit_mask(0)
            emit_scores(1)
            emit_out(0)
            emit_mask(1)
            emit_outevac(0)
            emit_scores(2)
            emit_out(1)
            emit_state2(0)
            emit_mask(2)
            emit_outevac(1)
            emit_scores(3)
            emit_out(2)
            emit_state2(1)
            emit_mask(3)
            emit_outevac(2)
            emit_out(3)
            emit_state2(2)
            emit_state2(3)
            emit_outevac(3)
            v_sqa[c] = op("ac", "activation", sqv, attn_sc, AF.Square, inc=s_ac)
            wt("pe", s_ac, v_sqa[c])
            for h in range(H):
                op("pe", "matmul", PS_C.ap()[0:1, 0:128], ONESC.ap(),
                   ap(bcS, 6144 + 128 * h, [[8192, 128], [1, 128]]),
                   start=(h == 0), stop=(h == H - 1),
                   inc=(s_pe if h == H - 1 else None), **MM)
            v_sqmm[c] = s_pe.n
            wt("ac", s_pe, v_sqmm[c])
            op("ac", "activation", GW.ap()[0:1, 16:144], PS_C.ap()[0:1, 0:128],
               AF.Ln, bias=EPSC.ap()[0:1, :], scale=1.0 / D)
            v_rmsrow[c] = op("ac", "activation", GW.ap()[0:1, 144:272],
                             GW.ap()[0:1, 16:144], AF.Exp, scale=-0.5, inc=s_ac)
            wt("pe", s_ac, v_rmsrow[c])
            op("pe", "matmul", PS_C.ap()[:, 512:640], GW.ap()[0:1, 304:432],
               GW.ap()[0:1, 144:272], start=True, stop=True, inc=s_pe, **MM)
            v_rstdbcmm[c] = s_pe.n
            wt("ac", s_pe, v_rstdbcmm[c])
            v_rstdsb[c] = op("ac", "copy", VPX.ap()[:, 0:128], PS_C.ap()[:, 512:640], inc=s_ac)
            wt("dv", s_ac, v_rstdsb[c])
            wt("dv", s_pe, v_p2state[c] if c < NCH - 1 else v_outmm[(c, 3)])
            op("dv", "tensor_mul",
               ap(SCR, 0, [[4096, 128], [128, 16], [1, 128]]),
               ap(SCR, 0, [[4096, 128], [128, 16], [1, 128]]),
               ap(VPX, 0, [[4096, 128], [0, 16], [1, 128]]))
            v_rms[c] = op("dv", "tensor_mul",
                          ap(bcB, 2048 * c, [[16384, 128], [1, 128], [128, 16]]),
                          ap(SCR, 0, [[4096, 128], [1, 128], [128, 16]]),
                          ap(RMSC, 0, [[16, 128], [0, 128], [1, 16]]),
                          inc=s_dv)
        barrier()
        if "attn" in dbg:
            tap("attn", ap(bcB, 0, [[16384, 128], [1, 16384]]))

        v_wmm, v_wevac, v_xsc, v_outst = {}, {}, {}, {}
        for m in range(16):
            if m >= 2:
                wt("po", s_pe, v_wmm[m - 2])
                v_wslw[m] = dma("po", dma_p,
                                WSL.ap()[:, 2048 * (m % 2):2048 * (m % 2) + 2048],
                                ap(wo, m * 128 * 2048, [[2048, 128], [1, 2048]]))
                v_bwsw[m] = dma("po", dma_p,
                                BWS.ap()[:, 512 * (m % 2):512 * (m % 2) + 512],
                                ap(bwm, m * 128 * 512, [[512, 128], [1, 512]]))
            if m >= 2:
                wt("sy", s_dv, v_wevac[m - 2])
            v_xsc[m] = dma("sy", dma_s,
                           VPX.ap()[:, 1024 * (m % 2):1024 * (m % 2) + 1024],
                           ap(xt, m * 128 * 1024, [[1024, 128], [1, 1024]]))
            wt("pe", dma_p, v_wslw[m])
            wt("pe", dma_p, v_bwsw[m])
            if m >= 2:
                wt("pe", s_dv, v_wevac[m - 2])
            for hf in range(2):
                pout = PS_A.ap()[:, 1024 * (m % 2) + 512 * hf:
                                 1024 * (m % 2) + 512 * hf + 512]
                for k in range(KD):
                    op("pe", "matmul", pout,
                       WSL.ap()[:, 2048 * (m % 2) + 128 * k:
                                2048 * (m % 2) + 128 * k + 128],
                       ap(bcB, 8192 * hf + 128 * k, [[16384, 128], [2048, 4], [1, 128]]),
                       start=(k == 0), stop=False, **MM)
                for k4 in range(4):
                    op("pe", "matmul", pout,
                       BWS.ap()[:, 512 * (m % 2) + 128 * k4:
                                512 * (m % 2) + 128 * k4 + 128],
                       HT.ap()[:, 1024 * k4 + 512 * hf:1024 * k4 + 512 * hf + 512],
                       start=False, stop=(k4 == 3),
                       inc=(s_pe if hf == 1 and k4 == 3 else None), **MM)
            v_wmm[m] = s_pe.n
            wt("dv", s_pe, v_wmm[m])
            wt("dv", dma_s, v_xsc[m])
            if m >= 2:
                wt("dv", dma_s, v_outst[m - 2])
            v_wevac[m] = op("dv", "tensor_add",
                            VPX.ap()[:, 2048 + 1024 * (m % 2):2048 + 1024 * (m % 2) + 1024],
                            PS_A.ap()[:, 1024 * (m % 2):1024 * (m % 2) + 1024],
                            VPX.ap()[:, 1024 * (m % 2):1024 * (m % 2) + 1024],
                            inc=s_dv)
            wt("sy", s_dv, v_wevac[m])
            v_outst[m] = dma("sy", dma_s,
                             ap(out_t, m * 128 * 1024, [[1024, 128], [1, 1024]]),
                             VPX.ap()[:, 2048 + 1024 * (m % 2):
                                      2048 + 1024 * (m % 2) + 1024])
        wt("sy", dma_s, v_outst[15])

        block = es.enter_context(nc.Block())

        @block.tensor
        def _(e):
            for f in ST["pe"]:
                f(e)

        @block.vector
        def _(e):
            for f in ST["dv"]:
                f(e)

        @block.scalar
        def _(e):
            for f in ST["ac"]:
                f(e)

        @block.gpsimd
        def _(e):
            for f in ST["po"]:
                f(e)

        @block.sync
        def _(e):
            for f in ST["sy"]:
                f(e)

    return nc


_CACHE = {}


def _get_nc():
    if "nc" not in _CACHE:
        _CACHE["nc"] = build_nc(())
    return _CACHE["nc"]


def kernel(L0_attn_x, L0_attn_Wq, L0_attn_Wk, L0_attn_Wv, L0_attn_Wo,
           L0_attn_rms_w, L0_ffn_A, L0_ffn_Bw, L0_ffn_Wg, _taps=None):
    x = np.asarray(L0_attn_x)
    wts = _host_weights(np.asarray(L0_attn_Wq), np.asarray(L0_attn_Wk),
                        np.asarray(L0_attn_Wv), np.asarray(L0_attn_Wo),
                        np.asarray(L0_attn_rms_w), np.asarray(L0_ffn_A),
                        np.asarray(L0_ffn_Bw), np.asarray(L0_ffn_Wg))
    consts = _host_constants()
    nc = build_nc(_taps) if _taps else _get_nc()
    in_maps = []
    for j in range(NCORES):
        m = {"xt": np.ascontiguousarray(x[0, j * NTOK:(j + 1) * NTOK, :].T)}
        m.update(wts)
        m.update(consts)
        m.update(_host_core_constants(j))
        in_maps.append(m)
    res = run_bass_kernel_spmd(nc, in_maps, core_ids=list(range(NCORES)))
    out = np.concatenate([res.results[j]["out"].T for j in range(NCORES)], axis=0)
    out = out.reshape(1, 8192, 2048).astype(np.float32)
    if _taps:
        return out, res
    return out


# version 02
# speedup vs baseline: 1.1133x; 1.1133x over previous
import contextlib

import ml_dtypes
import numpy as np

import concourse.bass as bass
import concourse.mybir as mybir
from concourse.bass_utils import run_bass_kernel_spmd

F32 = mybir.dt.float32
BF16 = mybir.dt.bfloat16
AF = mybir.ActivationFunctionType
ALU = mybir.AluOpType
AX = mybir.AxisListType
BFNP = ml_dtypes.bfloat16

NCORES = 8
D = 2048
H = 16
CH = 128
NCH = 8
NTOK = 1024
KD = 16
EPS = 1e-5


def _decay_rates():
    i = np.arange(H, dtype=np.float64)
    slopes = 2.0 ** (-(i + 1) / 2.0)
    return np.exp(-slopes)


def _host_constants():
    r = _decay_rates()
    c = np.arange(CH, dtype=np.float64)
    diff = c[None, :] - c[:, None]
    dm = np.where(diff >= 0, r[:, None, None] ** np.maximum(diff, 0.0)[None], 0.0)
    dmask = np.transpose(dm, (1, 0, 2)).reshape(CH, H * CH)
    qd = (r[:, None] ** (c[None, :] + 1.0))
    qdec = np.broadcast_to(qd.reshape(1, H * CH), (CH, H * CH))
    rci = np.zeros((CH, H * CH))
    for h in range(H):
        rci[:, h * CH:(h + 1) * CH] = np.eye(CH) * (r[h] ** CH)
    kdec = (r[None, :] ** (CH - 1.0 - c[:, None]))
    sw = np.zeros((CH, CH))
    for k in range(CH):
        sw[k, (k + 64) % CH] = 1.0
    return {
        "dmask": dmask.astype(BFNP),
        "qdec": np.ascontiguousarray(qdec).astype(BFNP),
        "rci": rci.astype(BFNP),
        "kdecc": kdec.astype(np.float32),
        "swapm": sw.astype(BFNP),
        "identb": np.eye(CH).astype(BFNP),
        "identf": np.eye(8).astype(np.float32),
    }


def _host_core_constants(core):
    r = _decay_rates()
    t = np.arange(NTOK, dtype=np.float64) + core * NTOK
    i = np.arange(32, dtype=np.float64)
    freq = (1.0 / 1024.0) ** (i / 31.0)
    theta = t[None, :] * freq[:, None]
    cosr = np.ones((CH, NTOK))
    sinr = np.zeros((CH, NTOK))
    cosr[0:32] = np.cos(theta)
    cosr[64:96] = np.cos(theta)
    sinr[0:32] = np.sin(theta)
    sinr[64:96] = -np.sin(theta)
    dec8 = r ** (CH * NCH)
    coef = np.zeros((CH, CH))
    for ii in range(NCORES):
        if ii < core:
            coef[:, 16 * ii:16 * ii + 16] = dec8[None, :] ** (core - 1 - ii)
    return {
        "cosr": cosr.astype(BFNP),
        "sinr": sinr.astype(BFNP),
        "coefc": coef.astype(np.float32),
    }


def _host_weights(Wq, Wk, Wv, Wo, rms_w, A, Bw, Wg):
    def m_major(W):
        return np.ascontiguousarray(
            W.reshape(16, 128, 16, 128).transpose(2, 1, 0, 3)).reshape(16, 128, 2048)
    wv_r = np.ascontiguousarray(
        Wv.reshape(16, 128, 4, 512).transpose(2, 1, 0, 3)).reshape(4, 128, 8192)
    a_r = np.ascontiguousarray(
        A.transpose(2, 0, 1).reshape(2048, 512).reshape(16, 128, 512)
        .transpose(1, 0, 2)).reshape(128, 8192)
    bwt = Bw.transpose(0, 2, 1).reshape(512, 2048)
    bw_r = np.ascontiguousarray(
        bwt.reshape(4, 128, 16, 128).transpose(2, 1, 0, 3)).reshape(16, 128, 512)
    wg_r = np.ascontiguousarray(
        Wg.T.reshape(16, 128, 8).transpose(1, 0, 2)).reshape(128, 128)
    rms_r = np.ascontiguousarray(rms_w.reshape(16, 128).T)
    return {
        "wq": m_major(Wq), "wk": m_major(Wk), "wo": m_major(Wo),
        "wv": wv_r, "aall": a_r, "bwm": bw_r, "wg": wg_r, "rmsw": rms_r,
    }


class Sem:
    def __init__(self, h):
        self.h = h
        self.n = 0

    def bump(self, k=1):
        self.n += k
        return self.n


def build_nc(taps=()):
    nc = bass.Bass("TRN2", target_bir_lowering=False,
                   dynamic_dma_scratch_size=12288)

    def din(name, shape, dt):
        return nc.dram_tensor(name, shape, dt, kind="ExternalInput")

    xt = din("xt", [D, NTOK], F32)
    wq = din("wq", [16, 128, 2048], F32)
    wk = din("wk", [16, 128, 2048], F32)
    wo = din("wo", [16, 128, 2048], F32)
    wv = din("wv", [4, 128, 8192], F32)
    aall = din("aall", [128, 8192], F32)
    bwm = din("bwm", [16, 128, 512], F32)
    wg = din("wg", [128, 128], F32)
    rmsw = din("rmsw", [128, 16], F32)
    dmaskd = din("dmask", [128, 2048], BF16)
    qdecd = din("qdec", [128, 2048], BF16)
    rcid = din("rci", [128, 2048], BF16)
    kdeccd = din("kdecc", [128, 16], F32)
    coefcd = din("coefc", [128, 128], F32)
    cosrd = din("cosr", [128, 1024], BF16)
    sinrd = din("sinr", [128, 1024], BF16)
    swapmd = din("swapm", [128, 128], BF16)
    identbd = din("identb", [128, 128], BF16)
    identfd = din("identf", [8, 8], F32)
    out_t = nc.dram_tensor("out", [D, NTOK], F32, kind="ExternalOutput")
    cc_in = nc.dram_tensor("cc_in", [128, 2048], F32)
    cc_out = nc.dram_tensor("cc_out", [1024, 2048], F32, addr_space="Shared")
    dbg = {}
    for tp in taps:
        shp, dt = taps[tp]
        dbg[tp] = nc.dram_tensor("dbg_" + tp, shp, dt, kind="ExternalOutput")

    def sb(name, shape, dt):
        return nc.alloc_sbuf_tensor(name, shape, dt)

    BUF_A = sb("buf_a", [128, 16384], F32)
    BUF_B = sb("buf_b", [128, 8192], F32)
    BUF_C = sb("buf_c", [128, 8192], F32)
    HT = sb("ht", [128, 4096], BF16)
    VPX = sb("vpx", [128, 4096], F32)
    SCR = sb("scr", [128, 4096], F32)
    WSL = sb("wsl", [128, 4096], BF16)
    BWS = sb("bws", [128, 1024], BF16)
    QD = sb("qd", [128, 2048], BF16)
    SCM = sb("scm", [128, 1024], BF16)
    STBF = sb("stbf", [128, 2048], BF16)
    DMASKC = sb("dmaskc", [128, 2048], BF16)
    QDECC = sb("qdecc", [128, 2048], BF16)
    RCIC = sb("rcic", [128, 2048], BF16)
    COSR = sb("cosrc", [128, 1024], BF16)
    SINR = sb("sinrc", [128, 1024], BF16)
    KDECC = sb("kdeccc", [128, 16], F32)
    RMSC = sb("rmsc", [128, 16], F32)
    COEFC = sb("coefcc", [128, 128], F32)
    WGT = sb("wgt", [128, 128], BF16)
    IDENTB = sb("identbc", [128, 128], BF16)
    IDENTF = sb("identfc", [8, 8], F32)
    SWAPM = sb("swapmc", [128, 128], BF16)
    ONESC = sb("onesc", [128, 1], BF16)
    EPSC = sb("epsc", [128, 1], F32)
    GW = sb("gw", [128, 432], F32)
    GEXP = sb("gexp", [128, 8], F32)

    PS_A = nc.alloc_psum_tensor("ps_a", [128, 2048], F32)
    PS_B = nc.alloc_psum_tensor("ps_b", [128, 1024], F32)
    PS_C = nc.alloc_psum_tensor("ps_c", [128, 1024], F32)

    bcA = BUF_A.ap().bitcast(BF16).tensor
    bcB = BUF_B.ap().bitcast(BF16).tensor
    bcC = BUF_C.ap().bitcast(BF16).tensor
    bcV = VPX.ap().bitcast(BF16).tensor
    bcS = SCR.ap().bitcast(BF16).tensor
    bcPB = PS_B.ap().bitcast(BF16).tensor
    bcPC = PS_C.ap().bitcast(BF16).tensor
    KT0 = 16384
    bcSTBF = STBF.ap().bitcast(F32).tensor
    bcSCM = SCM.ap().bitcast(F32).tensor

    def ap(t, col, dims, prow=0):
        pitch = t.shape[-1]
        return bass.AP(t, prow * pitch + col, dims)

    ST = {"pe": [], "dv": [], "ac": [], "po": [], "sy": []}

    def em(k, f):
        ST[k].append(f)

    with contextlib.ExitStack() as es:
        s_pe = Sem(es.enter_context(nc.semaphore("s_pe")))
        s_dv = Sem(es.enter_context(nc.semaphore("s_dv")))
        s_ac = Sem(es.enter_context(nc.semaphore("s_ac")))
        s_po = Sem(es.enter_context(nc.semaphore("s_po")))
        s_cc = Sem(es.enter_context(nc.semaphore("s_cc")))
        dma_p = Sem(es.enter_context(nc.semaphore("dma_p")))
        dma_s = Sem(es.enter_context(nc.semaphore("dma_s")))
        ALLSEMS = [s_pe, s_dv, s_ac, s_po, s_cc, dma_p, dma_s]
        MYSEM = {"pe": s_pe, "dv": s_dv, "ac": s_ac, "po": s_po, "sy": None}

        def wt(k, sem, v):
            if v and v > 0:
                em(k, lambda e, s=sem.h, vv=v: e.wait_ge(s, vv))

        def op(k, name, *a, inc=None, **kw):
            v = inc.bump(1) if inc is not None else None
            def f(e, name=name, a=a, kw=kw, inc=inc):
                ins = getattr(e, name)(*a, **kw)
                if inc is not None:
                    ins.then_inc(inc.h, 1)
            em(k, f)
            return v

        def dma(k, sem, out_ap, in_ap):
            v = sem.bump(16)
            em(k, lambda e, o=out_ap, i=in_ap, s=sem.h:
               e.dma_start(out=o, in_=i).then_inc(s, 16))
            return v

        def barrier():
            snap = [(s, s.n) for s in ALLSEMS]
            for k in ST:
                own = MYSEM[k]
                for s, v in snap:
                    if s is not own and v > 0:
                        wt(k, s, v)

        def tap(name, src_ap):
            if name in dbg:
                dma("sy", dma_s, dbg[name].ap(), src_ap)

        MM = dict(skip_group_check=True)

        x_in = ap(xt, 0, [[1024, 128], [131072, 16], [1, 1024]])
        x_out = ap(BUF_A, 0, [[16384, 128], [1024, 16], [1, 1024]])
        v_xt = dma("sy", dma_s, x_out, x_in)
        for dst, src in [(DMASKC, dmaskd), (QDECC, qdecd), (RCIC, rcid),
                         (COSR, cosrd), (SINR, sinrd), (KDECC, kdeccd),
                         (RMSC, rmsw), (COEFC, coefcd), (IDENTB, identbd),
                         (IDENTF, identfd), (SWAPM, swapmd)]:
            dma("sy", dma_s, dst.ap(), src.ap())
        v_syconsts = dma_s.n
        xb_out = ap(bcB, 0, [[16384, 128], [1024, 16], [1, 1024]])
        v_xb = dma("po", dma_p, xb_out, x_in)
        v_wg = dma("po", dma_p, WGT.ap(), wg.ap())
        v_aall = dma("po", dma_p, ap(bcV, 0, [[8192, 128], [1, 8192]]), aall.ap())
        op("po", "memset", ONESC.ap(), 1.0)
        op("po", "memset", GW.ap()[0:1, 304:432], 1.0)
        v_ms0 = op("po", "memset", EPSC.ap(), EPS, inc=s_po)
        wt("pe", dma_p, v_xb)
        wt("pe", s_po, v_ms0)
        wt("ac", dma_s, v_xt)
        wt("ac", s_po, v_ms0)
        wt("dv", s_po, v_ms0)

        for kd in range(KD):
            for hf in range(2):
                op("pe", "matmul", PS_B.ap()[0:1, 512 * hf:512 * hf + 512],
                   ONESC.ap(), ap(bcB, 1024 * kd + 512 * hf, [[16384, 128], [1, 512]]),
                   start=(kd == 0), stop=(kd == KD - 1),
                   inc=(s_pe if kd == KD - 1 and hf == 1 else None), **MM)
        v_sq1, v_sqmm1 = [], []
        for kd in range(KD):
            if kd >= 2:
                wt("ac", s_pe, v_sqmm1[kd - 2])
            v_sq1.append(op("ac", "activation",
                            QD.ap()[:, 1024 * (kd % 2):1024 * (kd % 2) + 1024],
                            BUF_A.ap()[:, 1024 * kd:1024 * kd + 1024],
                            AF.Square, inc=s_ac))
            wt("pe", s_ac, v_sq1[kd])
            for hf in range(2):
                op("pe", "matmul", PS_C.ap()[0:1, 512 * hf:512 * hf + 512],
                   ONESC.ap(),
                   QD.ap()[:, 1024 * (kd % 2) + 512 * hf:1024 * (kd % 2) + 512 * hf + 512],
                   start=(kd == 0), stop=(kd == KD - 1),
                   inc=(s_pe if hf == 1 else None), **MM)
            v_sqmm1.append(s_pe.n)
        v_stats = v_sqmm1[-1]

        row_mean = SCR.ap()[0:1, 1024:2048]
        row_rstd = SCR.ap()[0:1, 2048:3072]
        row_mrs = SCR.ap()[0:1, 3072:4096]
        row_tmp = SCR.ap()[0:1, 0:1024]
        wt("dv", s_pe, v_stats)
        op("dv", "tensor_scalar_mul", row_mean, PS_B.ap()[0:1, 0:1024], 1.0 / D)
        op("dv", "drain")
        op("dv", "tensor_mul", row_tmp, row_mean, row_mean)
        op("dv", "tensor_scalar_mul", row_rstd, PS_C.ap()[0:1, 0:1024], 1.0 / D)
        op("dv", "drain")
        v_var = op("dv", "tensor_sub", row_rstd, row_rstd, row_tmp, inc=s_dv)
        wt("ac", s_dv, v_var)
        op("ac", "activation", row_tmp, row_rstd, AF.Ln, bias=EPSC.ap()[0:1, :], scale=1.0)
        v_rstd = op("ac", "activation", row_rstd, row_tmp, AF.Exp, scale=-0.5, inc=s_ac)
        wt("dv", s_ac, v_rstd)
        v_mrs = op("dv", "tensor_mul", row_mrs, row_mean, row_rstd, inc=s_dv)
        wt("pe", s_dv, v_mrs)
        for hf in range(2):
            op("pe", "matmul", PS_B.ap()[:, 512 * hf:512 * hf + 512], GW.ap()[0:1, 304:432],
               SCR.ap()[0:1, 2048 + 512 * hf:2048 + 512 * hf + 512],
               start=True, stop=True, **MM)
            op("pe", "matmul", PS_C.ap()[:, 512 * hf:512 * hf + 512], GW.ap()[0:1, 304:432],
               SCR.ap()[0:1, 3072 + 512 * hf:3072 + 512 * hf + 512],
               start=True, stop=True, inc=(s_pe if hf == 1 else None), **MM)
        v_bcast = s_pe.n

        rstd_bc = ap(bcS, 0, [[8192, 128], [1, 1024]])
        mrs_bc = ap(bcS, 1024, [[8192, 128], [1, 1024]])
        wt("ac", s_pe, v_bcast)
        op("ac", "copy", rstd_bc, PS_B.ap()[:, 0:1024])
        v_bcsb = op("ac", "copy", mrs_bc, PS_C.ap()[:, 0:1024], inc=s_ac)
        xnf = ap(bcS, 2048, [[8192, 128], [1, 1024]])
        v_xn = []
        wt("dv", s_ac, v_bcsb)
        for kd in range(KD):
            op("dv", "tensor_mul", xnf,
               ap(bcB, 1024 * kd, [[16384, 128], [1, 1024]]), rstd_bc)
            v_xn.append(op("dv", "tensor_sub",
                           ap(bcC, 1024 * kd, [[16384, 128], [1, 1024]]),
                           xnf, mrs_bc, inc=s_dv))
        wt("pe", dma_p, v_wg)
        wt("pe", s_dv, v_xn[-1])
        for c in range(NCH):
            for kd in range(KD):
                op("pe", "matmul", PS_C.ap()[:, 8 * c:8 * c + 8],
                   ap(bcC, 1024 * kd + 128 * c, [[16384, 128], [1, 128]]),
                   WGT.ap()[:, 8 * kd:8 * kd + 8],
                   start=(kd == 0), stop=(kd == KD - 1),
                   inc=(s_pe if c == NCH - 1 and kd == KD - 1 else None), **MM)
        v_logitsmm = s_pe.n
        if "xn" in dbg:
            wt("sy", s_dv, v_xn[-1])
            tap("xn", ap(bcC, 0, [[16384, 128], [1, 16384]]))

        wt("dv", s_pe, v_logitsmm)
        op("dv", "tensor_copy", ap(bcSCM, 0, [[512, 128], [1, 64]]), PS_C.ap()[:, 0:64])
        for c in range(NCH):
            lg = ap(bcSCM, 8 * c, [[512, 128], [1, 8]])
            op("dv", "reduce_max", GW.ap()[:, 0:1], lg, AX.X)
            op("dv", "drain")
            v_g1 = op("dv", "tensor_scalar_mul", GW.ap()[:, 1:2], GW.ap()[:, 0:1],
                      -1.0, inc=s_dv)
            wt("ac", s_dv, v_g1)
            v_g2c = op("ac", "activation", GEXP.ap(), lg, AF.Exp,
                       bias=GW.ap()[:, 1:2], scale=1.0, inc=s_ac)
            wt("dv", s_ac, v_g2c)
            op("dv", "reduce_sum", GW.ap()[:, 2:3], GEXP.ap(), AX.X)
            op("dv", "drain")
            op("dv", "reciprocal", GW.ap()[:, 3:4], GW.ap()[:, 2:3])
            op("dv", "drain")
            op("dv", "tensor_scalar_mul", GW.ap()[:, 2:3], GW.ap()[:, 3:4], 4.0)
            op("dv", "tensor_single_scalar", GEXP.ap(), lg, GW.ap()[:, 0:1], ALU.is_equal)
            op("dv", "drain")
            op("dv", "tensor_scalar_mul", ap(bcSTBF, 8 * c, [[1024, 128], [1, 8]]), GEXP.ap(),
               GW.ap()[:, 2:3], inc=(s_dv if c == NCH - 1 else None))
        v_sgate = s_dv.n
        if "sgate" in dbg:
            wt("sy", s_dv, v_sgate)
            tap("sgate", ap(bcSTBF, 0, [[1024, 128], [1, 64]]))
            wt("ac", dma_s, dma_s.n)
        if "lgtm" in dbg:
            wt("sy", s_dv, v_sgate)
            tap("lgtm", ap(bcSCM, 0, [[512, 128], [1, 64]]))
            wt("dv", dma_s, dma_s.n)

        wt("pe", dma_p, v_aall)
        wt("pe", dma_s, v_syconsts)
        v_hmm, v_hevac, v_hTmm, v_hTevac = [], [], [], []
        for c in range(NCH):
            hp = PS_A.ap()[:, 1024 * (c % 2):1024 * (c % 2) + 512]
            if c >= 2:
                wt("pe", s_dv, v_hevac[c - 2])
            for k in range(KD):
                op("pe", "matmul", hp,
                   ap(bcC, 1024 * k + 128 * c, [[16384, 128], [1, 128]]),
                   ap(bcV, 512 * k, [[8192, 128], [1, 512]]),
                   start=(k == 0), stop=(k == KD - 1),
                   inc=(s_pe if k == KD - 1 else None), **MM)
            v_hmm.append(s_pe.n)
            wt("dv", s_pe, v_hmm[c])
            if c >= 1:
                wt("dv", s_pe, v_hTmm[c - 1])
            for ee in range(8):
                op("dv", "tensor_scalar_mul", SCM.ap()[:, 64 * ee:64 * ee + 64],
                   bass.AP(PS_A, 1024 * (c % 2) + 64 * ee, [[2048, 128], [1, 64]]),
                   ap(bcSTBF, 8 * c + ee, [[1024, 128], [1, 1]]),
                   inc=(s_dv if ee == 7 else None))
            v_hevac.append(s_dv.n)
            wt("pe", s_dv, v_hevac[c])
            if c == 0:
                wt("pe", s_dv, v_sgate)
            if c >= 1:
                wt("pe", s_dv, v_hTevac[c - 1])
            for b in range(4):
                op("pe", "transpose", ap(bcPC, 128 * b, [[2048, 128], [1, 128]]),
                   SCM.ap()[:, 128 * b:128 * b + 128], IDENTB.ap(),
                   inc=(s_pe if b == 3 else None))
            v_hTmm.append(s_pe.n)
            wt("dv", s_pe, v_hTmm[c])
            for b in range(4):
                op("dv", "tensor_copy",
                   HT.ap()[:, 1024 * b + 128 * c:1024 * b + 128 * c + 128],
                   ap(bcPC, 128 * b, [[2048, 128], [1, 128]]),
                   inc=(s_dv if b == 3 else None))
            v_hTevac.append(s_dv.n)
        if "hT" in dbg:
            wt("sy", s_dv, v_hTevac[-1])
            tap("hT", HT.ap())

        barrier()

        def projection(w_t, colbase):
            v_wsl, v_pmm, v_pevac, v_swpmm, v_swpevac, v_rot = {}, {}, {}, {}, {}, {}

            def swap_psum(m):
                return PS_B if (m % 2 == 0) else PS_C

            def emit_swap_mm(m):
                wt("pe", s_ac, v_pevac[m])
                if m >= 2:
                    wt("pe", s_ac, v_swpevac[m - 2])
                sp = swap_psum(m)
                for hf in range(2):
                    op("pe", "matmul", sp.ap()[:, 512 * hf:512 * hf + 512],
                       SWAPM.ap(),
                       ap(bcA, colbase + 1024 * m + 512 * hf, [[32768, 128], [1, 512]]),
                       start=True, stop=True, inc=(s_pe if hf == 1 else None), **MM)
                v_swpmm[m] = s_pe.n

            def emit_swp_evac(m):
                wt("ac", s_pe, v_swpmm[m])
                if m >= 2:
                    wt("ac", s_dv, v_rot[m - 2])
                v_swpevac[m] = op("ac", "copy",
                                  ap(bcV, 1024 * (m % 2), [[8192, 128], [1, 1024]]),
                                  swap_psum(m).ap()[:, 0:1024], inc=s_ac)

            def emit_rot(m):
                wt("dv", s_ac, v_swpevac[m])
                tile = ap(bcA, colbase + 1024 * m, [[32768, 128], [1, 1024]])
                rott = ap(bcS, 0, [[8192, 128], [1, 1024]])
                rotu = ap(bcS, 1024, [[8192, 128], [1, 1024]])
                op("dv", "tensor_mul", rott, tile, COSR.ap())
                op("dv", "tensor_mul", rotu,
                   ap(bcV, 1024 * (m % 2), [[8192, 128], [1, 1024]]), SINR.ap())
                v_rot[m] = op("dv", "tensor_add", tile, rott, rotu, inc=s_dv)

            for m in range(16):
                if m >= 2:
                    wt("po", s_pe, v_pmm[m - 2])
                v_wsl[m] = dma("po", dma_p,
                               WSL.ap()[:, 2048 * (m % 2):2048 * (m % 2) + 2048],
                               ap(w_t, m * 128 * 2048, [[2048, 128], [1, 2048]]))
                wt("pe", dma_p, v_wsl[m])
                if m >= 2:
                    wt("pe", s_ac, v_pevac[m - 2])
                for k in range(KD):
                    for hf in range(2):
                        op("pe", "matmul",
                           PS_A.ap()[:, 1024 * (m % 2) + 512 * hf:
                                     1024 * (m % 2) + 512 * hf + 512],
                           WSL.ap()[:, 2048 * (m % 2) + 128 * k:
                                    2048 * (m % 2) + 128 * k + 128],
                           ap(bcC, 1024 * k + 512 * hf, [[16384, 128], [1, 512]]),
                           start=(k == 0), stop=(k == KD - 1),
                           inc=(s_pe if k == KD - 1 and hf == 1 else None), **MM)
                v_pmm[m] = s_pe.n
                if m >= 1:
                    emit_swap_mm(m - 1)
                wt("ac", s_pe, v_pmm[m])
                v_pevac[m] = op("ac", "copy",
                                ap(bcA, colbase + 1024 * m, [[32768, 128], [1, 1024]]),
                                PS_A.ap()[:, 1024 * (m % 2):1024 * (m % 2) + 1024],
                                inc=s_ac)
                if m >= 1:
                    emit_swp_evac(m - 1)
                    emit_rot(m - 1)
            emit_swap_mm(15)
            emit_swp_evac(15)
            emit_rot(15)

        projection(wk, KT0)
        barrier()

        def v_psum(c):
            if c < 4:
                return bass.AP(PS_A, 512 * c, [[2048, 128], [1, 512]])
            if c < 6:
                return PS_B.ap()[:, 512 * (c - 4):512 * (c - 4) + 512]
            return PS_C.ap()[:, 512 * (c - 6):512 * (c - 6) + 512]

        v_vp, v_vhalf, v_vcmm, v_vevac = {}, {}, {}, {}
        for g in range(4):
            for hf in range(2):
                if g >= 1:
                    wt("po", s_pe, v_vhalf[(g - 1, hf)])
                v_vp[(g, hf)] = dma("po", dma_p,
                                    ap(bcV, 4096 * hf, [[8192, 128], [1, 4096]]),
                                    ap(wv, g * 128 * 8192 + 4096 * hf,
                                       [[8192, 128], [1, 4096]]))
            for hf in range(2):
                wt("pe", dma_p, v_vp[(g, hf)])
                if g >= 1 and hf == 0:
                    wt("pe", s_ac, v_vevac[(g - 1, 7)])
                for c in range(NCH):
                    for kk in range(8):
                        k = 8 * hf + kk
                        op("pe", "matmul", v_psum(c),
                           ap(bcC, 1024 * k + 128 * c, [[16384, 128], [1, 128]]),
                           ap(bcV, 4096 * hf + 512 * kk, [[8192, 128], [1, 512]]),
                           start=(k == 0), stop=(k == KD - 1),
                           inc=(s_pe if kk == 7 else None), **MM)
                    if hf == 1:
                        v_vcmm[(g, c)] = s_pe.n
                v_vhalf[(g, hf)] = s_pe.n
            for c in range(NCH):
                wt("ac", s_pe, v_vcmm[(g, c)])
                v_vevac[(g, c)] = op("ac", "copy",
                                     ap(bcB, 2048 * c + 512 * g, [[16384, 128], [1, 512]]),
                                     v_psum(c), inc=s_ac)
        barrier()
        if "v" in dbg:
            tap("v", ap(bcB, 0, [[16384, 128], [1, 16384]]))

        projection(wq, 0)
        barrier()
        if "qT" in dbg:
            tap("qT", ap(bcA, 0, [[32768, 128], [1, 16384]]))
        if "kT" in dbg:
            tap("kT", ap(bcA, KT0, [[32768, 128], [1, 16384]]))

        def kd_psum_ap(par, slot):
            return ap(bcPB, 1024 * par + 128 * slot, [[2048, 128], [1, 128]])

        v_kdT, v_kdevac = {}, {}
        v_p1state, v_stbf1 = {}, {}
        for c in range(NCH):
            if c > 0:
                wt("ac", s_pe, v_p1state[c - 1])
                v_stbf1[c] = op("ac", "copy", STBF.ap(), PS_A.ap()[:, 0:2048], inc=s_ac)

            def emit_kdT(G, c=c):
                gn = 4 * c + G
                if gn >= 2:
                    wt("pe", s_dv, v_kdevac[gn - 2])
                for h in range(4 * G, 4 * G + 4):
                    op("pe", "transpose", kd_psum_ap(gn % 2, h % 4),
                       ap(bcA, KT0 + 1024 * h + 128 * c, [[32768, 128], [1, 128]]),
                       IDENTB.ap(), inc=(s_pe if h % 4 == 3 else None))
                v_kdT[gn] = s_pe.n

            def emit_state(G, c=c):
                gn = 4 * c + G
                wt("pe", s_dv, v_kdevac[gn])
                if G == 0 and c > 0:
                    wt("pe", s_ac, v_stbf1[c])
                for h in range(4 * G, 4 * G + 4):
                    kd_sl = ap(bcC, 2048 * c + 128 * h, [[16384, 128], [1, 128]])
                    v_sl = ap(bcB, 2048 * c + 128 * h, [[16384, 128], [1, 128]])
                    st_out = PS_A.ap()[:, 128 * h:128 * h + 128]
                    last = (s_pe if h % 4 == 3 else None)
                    if c == 0:
                        op("pe", "matmul", st_out, kd_sl, v_sl, start=True, stop=True,
                           inc=last, **MM)
                    else:
                        op("pe", "matmul", st_out, RCIC.ap()[:, 128 * h:128 * h + 128],
                           STBF.ap()[:, 128 * h:128 * h + 128], start=True, stop=False,
                           **MM)
                        op("pe", "matmul", st_out, kd_sl, v_sl, start=False, stop=True,
                           inc=last, **MM)
                if G == 3:
                    v_p1state[c] = s_pe.n

            def emit_kdevac(G, c=c):
                gn = 4 * c + G
                wt("dv", s_pe, v_kdT[gn])
                for h in range(4 * G, 4 * G + 4):
                    op("dv", "tensor_scalar_mul",
                       ap(bcC, 2048 * c + 128 * h, [[16384, 128], [1, 128]]),
                       kd_psum_ap(gn % 2, h % 4), KDECC.ap()[:, h:h + 1],
                       inc=(s_dv if h % 4 == 3 else None))
                v_kdevac[gn] = s_dv.n

            emit_kdT(0)
            emit_kdevac(0)
            emit_kdT(1)
            emit_state(0)
            emit_kdevac(1)
            emit_kdT(2)
            emit_state(1)
            emit_kdevac(2)
            emit_kdT(3)
            emit_state(2)
            emit_kdevac(3)
            emit_state(3)

        wt("ac", s_pe, v_p1state[NCH - 1])
        v_Sevac = op("ac", "copy", VPX.ap()[:, 0:2048], PS_A.ap()[:, 0:2048], inc=s_ac)
        wt("sy", s_ac, v_Sevac)
        v_ccin = dma("sy", dma_s, cc_in.ap(), VPX.ap()[:, 0:2048])
        if "S" in dbg:
            tap("S", VPX.ap()[:, 0:2048])
        wt("po", dma_s, v_ccin)
        v_ccc = s_cc.bump()
        em("po", lambda e, s=s_cc.h: e.collective_compute(
            "AllGather", ALU.bypass, replica_groups=[list(range(NCORES))],
            ins=[cc_in.ap().opt()], outs=[cc_out.ap().opt()]).then_inc(s, 1))

        v_wslw, v_bwsw = {}, {}
        for m in range(2):
            v_wslw[m] = dma("po", dma_p,
                            WSL.ap()[:, 2048 * m:2048 * m + 2048],
                            ap(wo, m * 128 * 2048, [[2048, 128], [1, 2048]]))
            v_bwsw[m] = dma("po", dma_p,
                            BWS.ap()[:, 512 * m:512 * m + 512],
                            ap(bwm, m * 128 * 512, [[512, 128], [1, 512]]))

        v_gth, v_comb = {}, {}
        wt("sy", s_cc, v_ccc)
        for i in range(NCORES):
            if i >= 2:
                wt("sy", s_dv, v_comb.get(i - 2, 0))
            v_gth[i] = dma("sy", dma_s,
                           VPX.ap()[:, 2048 * (i % 2):2048 * (i % 2) + 2048],
                           ap(cc_out, i * 128 * 2048, [[2048, 128], [1, 2048]]))
            wt("dv", dma_s, v_gth[i])
            for h in range(H):
                gsl = VPX.ap()[:, 2048 * (i % 2) + 128 * h:2048 * (i % 2) + 128 * h + 128]
                ksl = SCR.ap()[:, 128 * h:128 * h + 128]
                co = COEFC.ap()[:, 16 * i + h:16 * i + h + 1]
                last = (s_dv if h == H - 1 else None)
                if i == 0:
                    op("dv", "tensor_scalar_mul", ksl, gsl, co, inc=last)
                else:
                    op("dv", "scalar_tensor_tensor", ksl, gsl, co, ksl,
                       ALU.mult, ALU.add, inc=last)
            v_comb[i] = s_dv.n
        wt("ac", s_dv, v_comb[NCORES - 1])
        kvin_bf = ap(bcS, 4096, [[8192, 128], [1, 2048]])
        v_kvinbf = op("ac", "copy", kvin_bf, SCR.ap()[:, 0:2048], inc=s_ac)
        if "KVin" in dbg:
            wt("sy", s_dv, v_comb[NCORES - 1])
            tap("KVin", SCR.ap()[:, 0:2048])
        barrier()

        attn_sc = SCR.ap()[:, 0:2048]
        sqv = ap(bcS, 6144, [[8192, 128], [1, 2048]])
        v_qd, v_stbf2, v_scores, v_maskevac, v_outmm = {}, {}, {}, {}, {}
        v_outevac, v_p2state, v_sqa, v_sqmm, v_rmsrow = {}, {}, {}, {}, {}
        v_rstdbcmm, v_rstdsb, v_rms = {}, {}, {}
        for c in range(NCH):
            if c > 0:
                wt("dv", s_pe, v_p2state[c - 1])
            v_qd[c] = op("dv", "tensor_mul",
                         ap(QD, 0, [[2048, 128], [128, 16], [1, 128]]),
                         ap(bcA, 128 * c, [[32768, 128], [1024, 16], [1, 128]]),
                         ap(QDECC, 0, [[2048, 128], [128, 16], [1, 128]]),
                         inc=s_dv)
            if c > 0:
                wt("ac", s_pe, v_p2state[c - 1])
                v_stbf2[c] = op("ac", "copy", STBF.ap(), PS_A.ap()[:, 0:2048], inc=s_ac)

            def emit_scores(G, c=c):
                gn = 4 * c + G
                if gn >= 2:
                    wt("pe", s_dv, v_maskevac[gn - 2])
                for h in range(4 * G, 4 * G + 4):
                    sl = PS_B.ap()[:, 512 * (gn % 2) + 128 * (h % 4):
                                   512 * (gn % 2) + 128 * (h % 4) + 128]
                    op("pe", "matmul", sl,
                       ap(bcA, KT0 + 1024 * h + 128 * c, [[32768, 128], [1, 128]]),
                       ap(bcA, 1024 * h + 128 * c, [[32768, 128], [1, 128]]),
                       start=True, stop=True, inc=(s_pe if h % 4 == 3 else None), **MM)
                v_scores[gn] = s_pe.n

            def emit_mask(G, c=c):
                gn = 4 * c + G
                wt("dv", s_pe, v_scores[gn])
                prevkey = (c, G - 2) if G >= 2 else (c - 1, G + 2)
                if prevkey in v_outmm:
                    wt("dv", s_pe, v_outmm[prevkey])
                op("dv", "tensor_mul", SCM.ap()[:, 512 * (gn % 2):512 * (gn % 2) + 512],
                   PS_B.ap()[:, 512 * (gn % 2):512 * (gn % 2) + 512],
                   DMASKC.ap()[:, 512 * G:512 * G + 512], inc=s_dv)
                v_maskevac[gn] = s_dv.n

            def emit_out(G, c=c):
                gn = 4 * c + G
                wt("pe", s_dv, v_maskevac[gn])
                if G == 0:
                    wt("pe", s_dv, v_qd[c])
                    if c > 0:
                        wt("pe", s_ac, v_stbf2[c])
                        wt("pe", s_ac, v_rmsrow[c - 1])
                    else:
                        wt("pe", s_ac, v_kvinbf)
                if G == 1 and c > 0:
                    wt("pe", s_ac, v_rstdsb[c - 1])
                prevkey = (c, G - 2) if G >= 2 else (c - 1, G + 2)
                if prevkey in v_outevac:
                    wt("pe", s_ac, v_outevac[prevkey])
                for h in range(4 * G, 4 * G + 4):
                    sl = PS_C.ap()[:, 512 * (gn % 2) + 128 * (h % 4):
                                   512 * (gn % 2) + 128 * (h % 4) + 128]
                    op("pe", "matmul", sl,
                       ap(bcB, 2048 * c + 128 * h, [[16384, 128], [1, 128]]),
                       SCM.ap()[:, 512 * (gn % 2) + 128 * (h % 4):
                                512 * (gn % 2) + 128 * (h % 4) + 128],
                       start=True, stop=False, **MM)
                    if c == 0:
                        stsl = ap(bcS, 4096 + 128 * h, [[8192, 128], [1, 128]])
                    else:
                        stsl = STBF.ap()[:, 128 * h:128 * h + 128]
                    op("pe", "matmul", sl, stsl, QD.ap()[:, 128 * h:128 * h + 128],
                       start=False, stop=True, inc=(s_pe if h % 4 == 3 else None), **MM)
                v_outmm[(c, G)] = s_pe.n

            def emit_state2(G, c=c):
                if c == NCH - 1:
                    return
                for h in range(4 * G, 4 * G + 4):
                    st_out = PS_A.ap()[:, 128 * h:128 * h + 128]
                    if c == 0:
                        ssl = ap(bcS, 4096 + 128 * h, [[8192, 128], [1, 128]])
                    else:
                        ssl = STBF.ap()[:, 128 * h:128 * h + 128]
                    op("pe", "matmul", st_out, RCIC.ap()[:, 128 * h:128 * h + 128],
                       ssl, start=True, stop=False, **MM)
                    op("pe", "matmul", st_out,
                       ap(bcC, 2048 * c + 128 * h, [[16384, 128], [1, 128]]),
                       ap(bcB, 2048 * c + 128 * h, [[16384, 128], [1, 128]]),
                       start=False, stop=True,
                       inc=(s_pe if G == 3 and h % 4 == 3 else None), **MM)
                if G == 3:
                    v_p2state[c] = s_pe.n

            def emit_outevac(G, c=c):
                gn = 4 * c + G
                wt("ac", s_pe, v_outmm[(c, G)])
                if G == 0 and c > 0:
                    wt("ac", s_dv, v_rms[c - 1])
                v_outevac[(c, G)] = op("ac", "copy",
                                       SCR.ap()[:, 512 * G:512 * G + 512],
                                       PS_C.ap()[:, 512 * (gn % 2):512 * (gn % 2) + 512],
                                       inc=s_ac)

            emit_scores(0)
            emit_mask(0)
            emit_scores(1)
            emit_out(0)
            emit_mask(1)
            emit_outevac(0)
            emit_scores(2)
            emit_out(1)
            emit_state2(0)
            emit_mask(2)
            emit_outevac(1)
            emit_scores(3)
            emit_out(2)
            emit_state2(1)
            emit_mask(3)
            emit_outevac(2)
            emit_out(3)
            emit_state2(2)
            emit_state2(3)
            emit_outevac(3)
            v_sqa[c] = op("ac", "activation", sqv, attn_sc, AF.Square, inc=s_ac)
            wt("pe", s_ac, v_sqa[c])
            for h in range(H):
                op("pe", "matmul", PS_C.ap()[0:1, 0:128], ONESC.ap(),
                   ap(bcS, 6144 + 128 * h, [[8192, 128], [1, 128]]),
                   start=(h == 0), stop=(h == H - 1),
                   inc=(s_pe if h == H - 1 else None), **MM)
            v_sqmm[c] = s_pe.n
            wt("ac", s_pe, v_sqmm[c])
            op("ac", "activation", GW.ap()[0:1, 16:144], PS_C.ap()[0:1, 0:128],
               AF.Ln, bias=EPSC.ap()[0:1, :], scale=1.0 / D)
            v_rmsrow[c] = op("ac", "activation", GW.ap()[0:1, 144:272],
                             GW.ap()[0:1, 16:144], AF.Exp, scale=-0.5, inc=s_ac)
            wt("pe", s_ac, v_rmsrow[c])
            op("pe", "matmul", PS_C.ap()[:, 512:640], GW.ap()[0:1, 304:432],
               GW.ap()[0:1, 144:272], start=True, stop=True, inc=s_pe, **MM)
            v_rstdbcmm[c] = s_pe.n
            wt("ac", s_pe, v_rstdbcmm[c])
            v_rstdsb[c] = op("ac", "copy", VPX.ap()[:, 0:128], PS_C.ap()[:, 512:640], inc=s_ac)
            wt("dv", s_ac, v_rstdsb[c])
            wt("dv", s_pe, v_p2state[c] if c < NCH - 1 else v_outmm[(c, 3)])
            op("dv", "tensor_mul",
               ap(SCR, 0, [[4096, 128], [128, 16], [1, 128]]),
               ap(SCR, 0, [[4096, 128], [128, 16], [1, 128]]),
               ap(VPX, 0, [[4096, 128], [0, 16], [1, 128]]))
            v_rms[c] = op("dv", "tensor_mul",
                          ap(bcB, 2048 * c, [[16384, 128], [1, 128], [128, 16]]),
                          ap(SCR, 0, [[4096, 128], [1, 128], [128, 16]]),
                          ap(RMSC, 0, [[16, 128], [0, 128], [1, 16]]),
                          inc=s_dv)
        barrier()
        if "attn" in dbg:
            tap("attn", ap(bcB, 0, [[16384, 128], [1, 16384]]))

        v_wmm, v_wevac, v_xsc, v_outst = {}, {}, {}, {}
        for m in range(16):
            if m >= 2:
                wt("po", s_pe, v_wmm[m - 2])
                v_wslw[m] = dma("po", dma_p,
                                WSL.ap()[:, 2048 * (m % 2):2048 * (m % 2) + 2048],
                                ap(wo, m * 128 * 2048, [[2048, 128], [1, 2048]]))
                v_bwsw[m] = dma("po", dma_p,
                                BWS.ap()[:, 512 * (m % 2):512 * (m % 2) + 512],
                                ap(bwm, m * 128 * 512, [[512, 128], [1, 512]]))
            if m >= 2:
                wt("sy", s_dv, v_wevac[m - 2])
            v_xsc[m] = dma("sy", dma_s,
                           VPX.ap()[:, 1024 * (m % 2):1024 * (m % 2) + 1024],
                           ap(xt, m * 128 * 1024, [[1024, 128], [1, 1024]]))
            wt("pe", dma_p, v_wslw[m])
            wt("pe", dma_p, v_bwsw[m])
            if m >= 2:
                wt("pe", s_dv, v_wevac[m - 2])
            for hf in range(2):
                pout = PS_A.ap()[:, 1024 * (m % 2) + 512 * hf:
                                 1024 * (m % 2) + 512 * hf + 512]
                for k in range(KD):
                    op("pe", "matmul", pout,
                       WSL.ap()[:, 2048 * (m % 2) + 128 * k:
                                2048 * (m % 2) + 128 * k + 128],
                       ap(bcB, 8192 * hf + 128 * k, [[16384, 128], [2048, 4], [1, 128]]),
                       start=(k == 0), stop=False, **MM)
                for k4 in range(4):
                    op("pe", "matmul", pout,
                       BWS.ap()[:, 512 * (m % 2) + 128 * k4:
                                512 * (m % 2) + 128 * k4 + 128],
                       HT.ap()[:, 1024 * k4 + 512 * hf:1024 * k4 + 512 * hf + 512],
                       start=False, stop=(k4 == 3),
                       inc=(s_pe if hf == 1 and k4 == 3 else None), **MM)
            v_wmm[m] = s_pe.n
            wt("dv", s_pe, v_wmm[m])
            wt("dv", dma_s, v_xsc[m])
            if m >= 2:
                wt("dv", dma_s, v_outst[m - 2])
            v_wevac[m] = op("dv", "tensor_add",
                            VPX.ap()[:, 2048 + 1024 * (m % 2):2048 + 1024 * (m % 2) + 1024],
                            PS_A.ap()[:, 1024 * (m % 2):1024 * (m % 2) + 1024],
                            VPX.ap()[:, 1024 * (m % 2):1024 * (m % 2) + 1024],
                            inc=s_dv)
            wt("sy", s_dv, v_wevac[m])
            v_outst[m] = dma("sy", dma_s,
                             ap(out_t, m * 128 * 1024, [[1024, 128], [1, 1024]]),
                             VPX.ap()[:, 2048 + 1024 * (m % 2):
                                      2048 + 1024 * (m % 2) + 1024])
        wt("sy", dma_s, v_outst[15])

        block = es.enter_context(nc.Block())

        @block.tensor
        def _(e):
            for f in ST["pe"]:
                f(e)

        @block.vector
        def _(e):
            for f in ST["dv"]:
                f(e)

        @block.scalar
        def _(e):
            for f in ST["ac"]:
                f(e)

        @block.gpsimd
        def _(e):
            for f in ST["po"]:
                f(e)

        @block.sync
        def _(e):
            for f in ST["sy"]:
                f(e)

    return nc


_CACHE = {}


def _get_nc():
    if "nc" not in _CACHE:
        _CACHE["nc"] = build_nc(())
    return _CACHE["nc"]


def kernel(L0_attn_x, L0_attn_Wq, L0_attn_Wk, L0_attn_Wv, L0_attn_Wo,
           L0_attn_rms_w, L0_ffn_A, L0_ffn_Bw, L0_ffn_Wg, _taps=None):
    x = np.asarray(L0_attn_x)
    wts = _host_weights(np.asarray(L0_attn_Wq), np.asarray(L0_attn_Wk),
                        np.asarray(L0_attn_Wv), np.asarray(L0_attn_Wo),
                        np.asarray(L0_attn_rms_w), np.asarray(L0_ffn_A),
                        np.asarray(L0_ffn_Bw), np.asarray(L0_ffn_Wg))
    consts = _host_constants()
    nc = build_nc(_taps) if _taps else _get_nc()
    in_maps = []
    for j in range(NCORES):
        m = {"xt": np.ascontiguousarray(x[0, j * NTOK:(j + 1) * NTOK, :].T)}
        m.update(wts)
        m.update(consts)
        m.update(_host_core_constants(j))
        in_maps.append(m)
    res = run_bass_kernel_spmd(nc, in_maps, core_ids=list(range(NCORES)))
    out = np.concatenate([res.results[j]["out"].T for j in range(NCORES)], axis=0)
    out = out.reshape(1, 8192, 2048).astype(np.float32)
    if _taps:
        return out, res
    return out


# version 09
# speedup vs baseline: 1.2060x; 1.0833x over previous
import contextlib

import ml_dtypes
import numpy as np

import concourse.bass as bass
import concourse.mybir as mybir
from concourse.bass_utils import run_bass_kernel_spmd

F32 = mybir.dt.float32
BF16 = mybir.dt.bfloat16
AF = mybir.ActivationFunctionType
ALU = mybir.AluOpType
AX = mybir.AxisListType
BFNP = ml_dtypes.bfloat16

NCORES = 8
D = 2048
H = 16
CH = 128
NCH = 8
NTOK = 1024
KD = 16
EPS = 1e-5


def _decay_rates():
    i = np.arange(H, dtype=np.float64)
    slopes = 2.0 ** (-(i + 1) / 2.0)
    return np.exp(-slopes)


def _host_constants():
    r = _decay_rates()
    c = np.arange(CH, dtype=np.float64)
    diff = c[None, :] - c[:, None]
    dm = np.where(diff >= 0, r[:, None, None] ** np.maximum(diff, 0.0)[None], 0.0)
    dmask = np.transpose(dm, (1, 0, 2)).reshape(CH, H * CH)
    qd = (r[:, None] ** (c[None, :] + 1.0))
    qdec = np.broadcast_to(qd.reshape(1, H * CH), (CH, H * CH))
    rci = np.zeros((CH, H * CH))
    for h in range(H):
        rci[:, h * CH:(h + 1) * CH] = np.eye(CH) * (r[h] ** CH)
    kdec = (r[None, :] ** (CH - 1.0 - c[:, None]))
    sw = np.zeros((CH, CH))
    for k in range(CH):
        sw[k, (k + 64) % CH] = 1.0
    return {
        "dmask": dmask.astype(BFNP),
        "qdec": np.ascontiguousarray(qdec).astype(BFNP),
        "rci": rci.astype(BFNP),
        "kdecc": kdec.astype(np.float32),
        "swapm": sw.astype(BFNP),
        "identb": np.eye(CH).astype(BFNP),
    }


def _host_core_constants(core):
    r = _decay_rates()
    t = np.arange(NTOK, dtype=np.float64) + core * NTOK
    i = np.arange(32, dtype=np.float64)
    freq = (1.0 / 1024.0) ** (i / 31.0)
    theta = t[None, :] * freq[:, None]
    cosr = np.ones((CH, NTOK))
    sinr = np.zeros((CH, NTOK))
    cosr[0:32] = np.cos(theta)
    cosr[64:96] = np.cos(theta)
    sinr[0:32] = np.sin(theta)
    sinr[64:96] = -np.sin(theta)
    dec8 = r ** (CH * NCH)
    coef = np.zeros((CH, CH))
    for ii in range(NCORES):
        if ii < core:
            coef[:, 16 * ii:16 * ii + 16] = dec8[None, :] ** (core - 1 - ii)
    return {
        "cosr": cosr.astype(BFNP),
        "sinr": sinr.astype(BFNP),
        "coefc": coef.astype(np.float32),
    }


def _host_weights(Wq, Wk, Wv, Wo, rms_w, A, Bw, Wg):
    def m_major(W):
        return np.ascontiguousarray(
            W.reshape(16, 128, 16, 128).transpose(2, 1, 0, 3)).reshape(16, 128, 2048)
    wv_r = np.ascontiguousarray(
        Wv.reshape(16, 128, 4, 512).transpose(2, 1, 0, 3)).reshape(4, 128, 8192)
    a_r = np.ascontiguousarray(
        A.transpose(2, 0, 1).reshape(2048, 512).reshape(16, 128, 512)
        .transpose(1, 0, 2)).reshape(128, 8192)
    bwt = Bw.transpose(0, 2, 1).reshape(512, 2048)
    bw_r = np.ascontiguousarray(
        bwt.reshape(4, 128, 16, 128).transpose(2, 1, 0, 3)).reshape(16, 128, 512)
    wg_r = np.ascontiguousarray(
        Wg.T.reshape(16, 128, 8).transpose(1, 0, 2)).reshape(128, 128)
    rms_r = np.ascontiguousarray(rms_w.reshape(16, 128).T)
    return {
        "wq": m_major(Wq), "wk": m_major(Wk), "wo": m_major(Wo),
        "wv": wv_r, "aall": a_r, "bwm": bw_r, "wg": wg_r, "rmsw": rms_r,
    }


class Sem:
    def __init__(self, h):
        self.h = h
        self.n = 0

    def bump(self, k=1):
        self.n += k
        return self.n


def build_nc(taps=()):
    nc = bass.Bass("TRN2", target_bir_lowering=False,
                   dynamic_dma_scratch_size=12288)

    def din(name, shape, dt):
        return nc.dram_tensor(name, shape, dt, kind="ExternalInput")

    xt = din("xt", [D, NTOK], F32)
    wq = din("wq", [16, 128, 2048], F32)
    wk = din("wk", [16, 128, 2048], F32)
    wo = din("wo", [16, 128, 2048], F32)
    wv = din("wv", [4, 128, 8192], F32)
    aall = din("aall", [128, 8192], F32)
    bwm = din("bwm", [16, 128, 512], F32)
    wg = din("wg", [128, 128], F32)
    rmsw = din("rmsw", [128, 16], F32)
    dmaskd = din("dmask", [128, 2048], BF16)
    qdecd = din("qdec", [128, 2048], BF16)
    rcid = din("rci", [128, 2048], BF16)
    kdeccd = din("kdecc", [128, 16], F32)
    coefcd = din("coefc", [128, 128], F32)
    cosrd = din("cosr", [128, 1024], BF16)
    sinrd = din("sinr", [128, 1024], BF16)
    swapmd = din("swapm", [128, 128], BF16)
    identbd = din("identb", [128, 128], BF16)
    out_t = nc.dram_tensor("out", [D, NTOK], F32, kind="ExternalOutput")
    cc_in = nc.dram_tensor("cc_in", [128, 2048], BF16)
    cc_out = nc.dram_tensor("cc_out", [1024, 2048], BF16, addr_space="Shared")
    dbg = {}
    for tp in taps:
        shp, dt = taps[tp]
        dbg[tp] = nc.dram_tensor("dbg_" + tp, shp, dt, kind="ExternalOutput")

    def sb(name, shape, dt):
        return nc.alloc_sbuf_tensor(name, shape, dt)

    BUF_A = sb("buf_a", [128, 16384], F32)
    BUF_B = sb("buf_b", [128, 8192], F32)
    BUF_C = sb("buf_c", [128, 8192], F32)
    HT = sb("ht", [128, 4096], BF16)
    VPX = sb("vpx", [128, 4096], F32)
    SCR = sb("scr", [128, 4096], F32)
    WSL = sb("wsl", [128, 4096], BF16)
    BWS = sb("bws", [128, 1024], BF16)
    QD = sb("qd", [128, 2048], BF16)
    SCM = sb("scm", [128, 1024], BF16)
    STBF = sb("stbf", [128, 2048], BF16)
    DMASKC = sb("dmaskc", [128, 2048], BF16)
    QDECC = sb("qdecc", [128, 2048], BF16)
    RCIC = sb("rcic", [128, 2048], BF16)
    COSR = sb("cosrc", [128, 1024], BF16)
    SINR = sb("sinrc", [128, 1024], BF16)
    KDECC = sb("kdeccc", [128, 16], F32)
    RMSC = sb("rmsc", [128, 16], F32)
    COEFC = sb("coefcc", [128, 128], F32)
    WGT = sb("wgt", [128, 128], BF16)
    IDENTB = sb("identbc", [128, 128], BF16)
    SWAPM = sb("swapmc", [128, 128], BF16)
    ONESC = sb("onesc", [128, 1], BF16)
    EPSC = sb("epsc", [128, 1], F32)
    GW = sb("gw", [128, 496], F32)
    GEXP = sb("gexp", [128, 8], F32)

    PS_A = nc.alloc_psum_tensor("ps_a", [128, 2048], F32)
    PS_B = nc.alloc_psum_tensor("ps_b", [128, 1024], F32)
    PS_C = nc.alloc_psum_tensor("ps_c", [128, 1024], F32)

    bcA = BUF_A.ap().bitcast(BF16).tensor
    bcB = BUF_B.ap().bitcast(BF16).tensor
    bcC = BUF_C.ap().bitcast(BF16).tensor
    bcV = VPX.ap().bitcast(BF16).tensor
    bcS = SCR.ap().bitcast(BF16).tensor
    bcPB = PS_B.ap().bitcast(BF16).tensor
    bcPC = PS_C.ap().bitcast(BF16).tensor
    KT0 = 16384
    bcSTBF = STBF.ap().bitcast(F32).tensor
    bcSCM = SCM.ap().bitcast(F32).tensor

    def ap(t, col, dims, prow=0):
        pitch = t.shape[-1]
        return bass.AP(t, prow * pitch + col, dims)

    ST = {"pe": [], "dv": [], "ac": [], "po": [], "sy": []}

    def em(k, f):
        ST[k].append(f)

    with contextlib.ExitStack() as es:
        s_pe = Sem(es.enter_context(nc.semaphore("s_pe")))
        s_dv = Sem(es.enter_context(nc.semaphore("s_dv")))
        s_ac = Sem(es.enter_context(nc.semaphore("s_ac")))
        s_po = Sem(es.enter_context(nc.semaphore("s_po")))
        s_cc = Sem(es.enter_context(nc.semaphore("s_cc")))
        dma_p = Sem(es.enter_context(nc.semaphore("dma_p")))
        dma_s = Sem(es.enter_context(nc.semaphore("dma_s")))
        ALLSEMS = [s_pe, s_dv, s_ac, s_po, s_cc, dma_p, dma_s]
        MYSEM = {"pe": s_pe, "dv": s_dv, "ac": s_ac, "po": s_po, "sy": None}

        def wt(k, sem, v):
            if v and v > 0:
                em(k, lambda e, s=sem.h, vv=v: e.wait_ge(s, vv))

        def op(k, name, *a, inc=None, **kw):
            v = inc.bump(1) if inc is not None else None
            def f(e, name=name, a=a, kw=kw, inc=inc):
                ins = getattr(e, name)(*a, **kw)
                if inc is not None:
                    ins.then_inc(inc.h, 1)
            em(k, f)
            return v

        def dma(k, sem, out_ap, in_ap, transpose=False):
            v = sem.bump(16)
            em(k, lambda e, o=out_ap, i=in_ap, s=sem.h, tr=transpose:
               e.dma_start(out=o, in_=i, transpose=tr).then_inc(s, 16))
            return v

        def barrier():
            snap = [(s, s.n) for s in ALLSEMS]
            for k in ST:
                own = MYSEM[k]
                for s, v in snap:
                    if s is not own and v > 0:
                        wt(k, s, v)

        def tap(name, src_ap):
            if name in dbg:
                dma("sy", dma_s, dbg[name].ap(), src_ap)

        MM = dict(skip_group_check=True)

        for dst, src in [(DMASKC, dmaskd), (QDECC, qdecd), (RCIC, rcid),
                         (COSR, cosrd), (SINR, sinrd), (KDECC, kdeccd),
                         (RMSC, rmsw), (COEFC, coefcd), (IDENTB, identbd),
                         (SWAPM, swapmd)]:
            dma("sy", dma_s, dst.ap(), src.ap())
        v_syconsts = dma_s.n
        op("dv", "memset", ONESC.ap(), 1.0)
        op("dv", "memset", GW.ap()[0:1, 304:432], 1.0)
        v_ms0 = op("dv", "memset", EPSC.ap(), EPS, inc=s_dv)
        v_xbt = []
        for kd in range(KD):
            v_xbt.append(dma("po", dma_p,
                             ap(bcB, 1024 * kd, [[16384, 128], [1, 1024]]),
                             ap(xt, kd * 128 * 1024, [[1024, 128], [1, 1024]])))
        v_wg = dma("po", dma_p, WGT.ap(), wg.ap())
        wt("pe", s_dv, v_ms0)
        wt("ac", s_dv, v_ms0)

        v_sq1, v_sqmm1 = [], []
        for kd in range(KD):
            wt("pe", dma_p, v_xbt[kd])
            for hf in range(2):
                op("pe", "matmul", PS_B.ap()[0:1, 512 * hf:512 * hf + 512],
                   ONESC.ap(), ap(bcB, 1024 * kd + 512 * hf, [[16384, 128], [1, 512]]),
                   start=(kd == 0), stop=(kd == KD - 1),
                   inc=(s_pe if kd == KD - 1 and hf == 1 else None), **MM)
        for kd in range(KD):
            if kd >= 2:
                wt("ac", s_pe, v_sqmm1[kd - 2])
            wt("ac", dma_p, v_xbt[kd])
            v_sq1.append(op("ac", "activation",
                            QD.ap()[:, 1024 * (kd % 2):1024 * (kd % 2) + 1024],
                            ap(bcB, 1024 * kd, [[16384, 128], [1, 1024]]),
                            AF.Square, inc=s_ac))
            wt("pe", s_ac, v_sq1[kd])
            for hf in range(2):
                op("pe", "matmul", PS_C.ap()[0:1, 512 * hf:512 * hf + 512],
                   ONESC.ap(),
                   QD.ap()[:, 1024 * (kd % 2) + 512 * hf:1024 * (kd % 2) + 512 * hf + 512],
                   start=(kd == 0), stop=(kd == KD - 1),
                   inc=(s_pe if hf == 1 else None), **MM)
            v_sqmm1.append(s_pe.n)
        v_stats = v_sqmm1[-1]

        row_mean = SCR.ap()[0:1, 1024:2048]
        row_rstd = SCR.ap()[0:1, 2048:3072]
        row_mrs = SCR.ap()[0:1, 3072:4096]
        row_tmp = SCR.ap()[0:1, 0:1024]
        wt("dv", s_pe, v_stats)
        op("dv", "tensor_scalar_mul", row_mean, PS_B.ap()[0:1, 0:1024], 1.0 / D)
        op("dv", "drain")
        op("dv", "tensor_mul", row_tmp, row_mean, row_mean)
        op("dv", "tensor_scalar_mul", row_rstd, PS_C.ap()[0:1, 0:1024], 1.0 / D)
        op("dv", "drain")
        v_var = op("dv", "tensor_sub", row_rstd, row_rstd, row_tmp, inc=s_dv)
        wt("ac", s_dv, v_var)
        op("ac", "activation", row_tmp, row_rstd, AF.Ln, bias=EPSC.ap()[0:1, :], scale=1.0)
        v_rstd = op("ac", "activation", row_rstd, row_tmp, AF.Exp, scale=-0.5, inc=s_ac)
        wt("dv", s_ac, v_rstd)
        v_mrs = op("dv", "tensor_mul", row_mrs, row_mean, row_rstd, inc=s_dv)
        wt("pe", s_dv, v_mrs)
        for hf in range(2):
            op("pe", "matmul", PS_B.ap()[:, 512 * hf:512 * hf + 512],
               GW.ap()[0:1, 304:432],
               SCR.ap()[0:1, 2048 + 512 * hf:2048 + 512 * hf + 512],
               start=True, stop=True, **MM)
            op("pe", "matmul", PS_C.ap()[:, 512 * hf:512 * hf + 512],
               GW.ap()[0:1, 304:432],
               SCR.ap()[0:1, 3072 + 512 * hf:3072 + 512 * hf + 512],
               start=True, stop=True, inc=(s_pe if hf == 1 else None), **MM)
        v_bcast = s_pe.n

        rstd_bc = ap(bcS, 0, [[8192, 128], [1, 1024]])
        mrs_bc = ap(bcS, 1024, [[8192, 128], [1, 1024]])
        wt("ac", s_pe, v_bcast)
        op("ac", "copy", rstd_bc, PS_B.ap()[:, 0:1024])
        v_bcsb = op("ac", "copy", mrs_bc, PS_C.ap()[:, 0:1024], inc=s_ac)
        xnf = ap(bcS, 2048, [[8192, 128], [1, 1024]])
        v_xn = []
        wt("dv", s_ac, v_bcsb)
        for kd in range(KD):
            op("dv", "tensor_mul", xnf,
               ap(bcB, 1024 * kd, [[16384, 128], [1, 1024]]), rstd_bc)
            v_xn.append(op("dv", "tensor_sub",
                           ap(bcC, 1024 * kd, [[16384, 128], [1, 1024]]),
                           xnf, mrs_bc, inc=s_dv))
        wt("pe", dma_p, v_wg)
        for c in range(NCH):
            for kd in range(KD):
                if c == 0:
                    wt("pe", s_dv, v_xn[kd])
                op("pe", "matmul", PS_C.ap()[:, 8 * c:8 * c + 8],
                   ap(bcC, 1024 * kd + 128 * c, [[16384, 128], [1, 128]]),
                   WGT.ap()[:, 8 * kd:8 * kd + 8],
                   start=(kd == 0), stop=(kd == KD - 1),
                   inc=(s_pe if c == NCH - 1 and kd == KD - 1 else None), **MM)
        v_logitsmm = s_pe.n
        if "xn" in dbg:
            wt("sy", s_dv, v_xn[-1])
            tap("xn", ap(bcC, 0, [[16384, 128], [1, 16384]]))

        wt("dv", s_pe, v_logitsmm)
        v_lgtm = op("dv", "tensor_copy", ap(bcSCM, 0, [[512, 128], [1, 64]]),
                    PS_C.ap()[:, 0:64], inc=s_dv)
        for c in range(NCH):
            lg = ap(bcSCM, 8 * c, [[512, 128], [1, 8]])
            op("dv", "reduce_max", GW.ap()[:, 0:1], lg, AX.X)
            op("dv", "drain")
            v_g1 = op("dv", "tensor_scalar_mul", GW.ap()[:, 1:2], GW.ap()[:, 0:1],
                      -1.0, inc=s_dv)
            wt("ac", s_dv, v_g1)
            v_g2c = op("ac", "activation", GEXP.ap(), lg, AF.Exp,
                       bias=GW.ap()[:, 1:2], scale=1.0, inc=s_ac)
            wt("dv", s_ac, v_g2c)
            op("dv", "reduce_sum", GW.ap()[:, 2:3], GEXP.ap(), AX.X)
            op("dv", "drain")
            op("dv", "reciprocal", GW.ap()[:, 3:4], GW.ap()[:, 2:3])
            op("dv", "drain")
            op("dv", "tensor_scalar_mul", GW.ap()[:, 2:3], GW.ap()[:, 3:4], 4.0)
            op("dv", "tensor_single_scalar", GEXP.ap(), lg, GW.ap()[:, 0:1], ALU.is_equal)
            op("dv", "drain")
            op("dv", "tensor_scalar_mul", GW.ap()[:, 432 + 8 * c:440 + 8 * c],
               GEXP.ap(), GW.ap()[:, 2:3], inc=(s_dv if c == NCH - 1 else None))
        v_sgate = s_dv.n
        if "sgate" in dbg:
            wt("sy", s_dv, v_sgate)
            tap("sgate", GW.ap()[:, 432:496])
            wt("ac", dma_s, dma_s.n)

        def projection(w_t, colbase, swap_extra_waits=()):
            v_wsl, v_pmm, v_pevac, v_swpmm, v_swpevac, v_rot = {}, {}, {}, {}, {}, {}

            def swap_psum(m):
                return PS_B if (m % 2 == 0) else PS_C

            def emit_swap_mm(m):
                wt("pe", s_ac, v_pevac[m])
                if m <= 1:
                    for semx, vv in swap_extra_waits:
                        wt("pe", semx, vv)
                if m >= 2:
                    wt("pe", s_ac, v_swpevac[m - 2])
                sp = swap_psum(m)
                for hf in range(2):
                    op("pe", "matmul", sp.ap()[:, 512 * hf:512 * hf + 512],
                       SWAPM.ap(),
                       ap(bcA, colbase + 1024 * m + 512 * hf, [[32768, 128], [1, 512]]),
                       start=True, stop=True, inc=(s_pe if hf == 1 else None), **MM)
                v_swpmm[m] = s_pe.n

            def emit_swp_evac(m):
                wt("ac", s_pe, v_swpmm[m])
                if m >= 2:
                    wt("ac", s_dv, v_rot[m - 2])
                v_swpevac[m] = op("ac", "copy",
                                  QD.ap()[:, 1024 * (m % 2):1024 * (m % 2) + 1024],
                                  swap_psum(m).ap()[:, 0:1024], inc=s_ac)

            def emit_rot(m):
                wt("dv", s_ac, v_swpevac[m])
                tile = ap(bcA, colbase + 1024 * m, [[32768, 128], [1, 1024]])
                rott = ap(bcS, 0, [[8192, 128], [1, 1024]])
                rotu = ap(bcS, 1024, [[8192, 128], [1, 1024]])
                op("dv", "tensor_mul", rott, tile, COSR.ap())
                op("dv", "tensor_mul", rotu,
                   QD.ap()[:, 1024 * (m % 2):1024 * (m % 2) + 1024], SINR.ap())
                v_rot[m] = op("dv", "tensor_add", tile, rott, rotu, inc=s_dv)

            for m in range(16):
                if m >= 2:
                    wt("po", s_pe, v_pmm[m - 2])
                v_wsl[m] = dma("po", dma_p,
                               WSL.ap()[:, 2048 * (m % 2):2048 * (m % 2) + 2048],
                               ap(w_t, m * 128 * 2048, [[2048, 128], [1, 2048]]))
                wt("pe", dma_p, v_wsl[m])
                if m >= 2:
                    wt("pe", s_ac, v_pevac[m - 2])
                for k in range(KD):
                    for hf in range(2):
                        op("pe", "matmul",
                           PS_A.ap()[:, 1024 * (m % 2) + 512 * hf:
                                     1024 * (m % 2) + 512 * hf + 512],
                           WSL.ap()[:, 2048 * (m % 2) + 128 * k:
                                    2048 * (m % 2) + 128 * k + 128],
                           ap(bcC, 1024 * k + 512 * hf, [[16384, 128], [1, 512]]),
                           start=(k == 0), stop=(k == KD - 1),
                           inc=(s_pe if k == KD - 1 and hf == 1 else None), **MM)
                v_pmm[m] = s_pe.n
                if m >= 1:
                    emit_swap_mm(m - 1)
                wt("ac", s_pe, v_pmm[m])
                v_pevac[m] = op("ac", "copy",
                                ap(bcA, colbase + 1024 * m, [[32768, 128], [1, 1024]]),
                                PS_A.ap()[:, 1024 * (m % 2):1024 * (m % 2) + 1024],
                                inc=s_ac)
                if m >= 1:
                    emit_swp_evac(m - 1)
                    emit_rot(m - 1)
            emit_swap_mm(15)
            emit_swp_evac(15)
            emit_rot(15)
            return v_pmm, v_pevac, v_swpevac, v_rot

        vK_pmm, vK_pevac, vK_swpevac, vK_rot = projection(
            wk, KT0, swap_extra_waits=((s_ac, v_bcsb), (s_dv, v_lgtm)))

        def v_psum(c):
            if c < 4:
                return bass.AP(PS_A, 512 * c, [[2048, 128], [1, 512]])
            if c < 6:
                return PS_B.ap()[:, 512 * (c - 4):512 * (c - 4) + 512]
            return PS_C.ap()[:, 512 * (c - 6):512 * (c - 6) + 512]

        v_vp, v_vhalf, v_vcmm, v_vevac = {}, {}, {}, {}
        for g in range(4):
            for hf in range(2):
                if g >= 1:
                    wt("po", s_pe, v_vhalf[(g - 1, hf)])
                v_vp[(g, hf)] = dma("po", dma_p,
                                    ap(bcV, 4096 * hf, [[8192, 128], [1, 4096]]),
                                    ap(wv, g * 128 * 8192 + 4096 * hf,
                                       [[8192, 128], [1, 4096]]))
            for hf in range(2):
                wt("pe", dma_p, v_vp[(g, hf)])
                if g == 0 and hf == 0:
                    wt("pe", s_ac, vK_swpevac[15])
                if g >= 1 and hf == 0:
                    wt("pe", s_ac, v_vevac[(g - 1, 7)])
                for c in range(NCH):
                    for kk in range(8):
                        k = 8 * hf + kk
                        op("pe", "matmul", v_psum(c),
                           ap(bcC, 1024 * k + 128 * c, [[16384, 128], [1, 128]]),
                           ap(bcV, 4096 * hf + 512 * kk, [[8192, 128], [1, 512]]),
                           start=(k == 0), stop=(k == KD - 1),
                           inc=(s_pe if kk == 7 else None), **MM)
                    if hf == 1:
                        v_vcmm[(g, c)] = s_pe.n
                v_vhalf[(g, hf)] = s_pe.n
            for c in range(NCH):
                wt("ac", s_pe, v_vcmm[(g, c)])
                v_vevac[(g, c)] = op("ac", "copy",
                                     ap(bcB, 2048 * c + 512 * g, [[16384, 128], [1, 512]]),
                                     v_psum(c), inc=s_ac)
        barrier()
        if "v" in dbg:
            tap("v", ap(bcB, 0, [[16384, 128], [1, 16384]]))
        if "kT" in dbg:
            tap("kT", ap(bcA, KT0, [[32768, 128], [1, 16384]]))

        def kd_psum_ap(par, slot):
            return ap(bcPB, 1024 * par + 128 * slot, [[2048, 128], [1, 128]])

        v_kdT, v_kdevac = {}, {}
        v_p1state, v_p1stG, v_stbf1 = {}, {}, {}
        for c in range(NCH):
            if c > 0:
                wt("ac", s_pe, v_p1state[c - 1])
                v_stbf1[c] = op("ac", "copy", STBF.ap(), PS_A.ap()[:, 0:2048], inc=s_ac)

            def emit_kdT(G, c=c):
                gn = 4 * c + G
                if gn >= 2:
                    wt("pe", s_dv, v_kdevac[gn - 2])
                for h in range(4 * G, 4 * G + 4):
                    op("pe", "transpose", kd_psum_ap(gn % 2, h % 4),
                       ap(bcA, KT0 + 1024 * h + 128 * c, [[32768, 128], [1, 128]]),
                       IDENTB.ap(), inc=(s_pe if h % 4 == 3 else None))
                v_kdT[gn] = s_pe.n

            def emit_state(G, c=c):
                gn = 4 * c + G
                wt("pe", s_dv, v_kdevac[gn])
                if G == 0 and c > 0:
                    wt("pe", s_ac, v_stbf1[c])
                for h in range(4 * G, 4 * G + 4):
                    kd_sl = QD.ap()[:, 512 * (gn % 2) + 128 * (h % 4):
                                    512 * (gn % 2) + 128 * (h % 4) + 128]
                    v_sl = ap(bcB, 2048 * c + 128 * h, [[16384, 128], [1, 128]])
                    st_out = PS_A.ap()[:, 128 * h:128 * h + 128]
                    last = (s_pe if h % 4 == 3 else None)
                    if c == 0:
                        op("pe", "matmul", st_out, kd_sl, v_sl, start=True, stop=True,
                           inc=last, **MM)
                    else:
                        op("pe", "matmul", st_out, RCIC.ap()[:, 128 * h:128 * h + 128],
                           STBF.ap()[:, 128 * h:128 * h + 128], start=True, stop=False,
                           **MM)
                        op("pe", "matmul", st_out, kd_sl, v_sl, start=False, stop=True,
                           inc=last, **MM)
                v_p1stG[gn] = s_pe.n
                if G == 3:
                    v_p1state[c] = s_pe.n

            def emit_kdevac(G, c=c):
                gn = 4 * c + G
                wt("dv", s_pe, v_kdT[gn])
                if gn >= 2:
                    wt("dv", s_pe, v_p1stG[gn - 2])
                for h in range(4 * G, 4 * G + 4):
                    op("dv", "tensor_scalar_mul",
                       QD.ap()[:, 512 * (gn % 2) + 128 * (h % 4):
                               512 * (gn % 2) + 128 * (h % 4) + 128],
                       kd_psum_ap(gn % 2, h % 4), KDECC.ap()[:, h:h + 1],
                       inc=(s_dv if h % 4 == 3 else None))
                v_kdevac[gn] = s_dv.n

            emit_kdT(0)
            emit_kdevac(0)
            emit_kdT(1)
            emit_state(0)
            emit_kdevac(1)
            emit_kdT(2)
            emit_state(1)
            emit_kdevac(2)
            emit_kdT(3)
            emit_state(2)
            emit_kdevac(3)
            emit_state(3)

        wt("ac", s_pe, v_p1state[NCH - 1])
        v_Sevac = op("ac", "copy", STBF.ap(), PS_A.ap()[:, 0:2048], inc=s_ac)
        wt("sy", s_ac, v_Sevac)
        v_ccin = dma("sy", dma_s, cc_in.ap(), STBF.ap())
        if "S" in dbg:
            tap("S", STBF.ap())
        wt("po", dma_s, v_ccin)
        v_ccc = s_cc.bump()
        em("po", lambda e, s=s_cc.h: e.collective_compute(
            "AllGather", ALU.bypass, replica_groups=[list(range(NCORES))],
            ins=[cc_in.ap().opt()], outs=[cc_out.ap().opt()]).then_inc(s, 1))

        wt("po", s_pe, vK_pmm[15])
        wt("pe", s_ac, v_Sevac)
        wt("pe", s_dv, v_kdevac[4 * NCH - 1])
        vQ_pmm, vQ_pevac, vQ_swpevac, vQ_rot = projection(wq, 0)

        def h_psum(c):
            if c < 4:
                return bass.AP(PS_A, 512 * c, [[2048, 128], [1, 512]])
            if c < 6:
                return PS_B.ap()[:, 512 * (c - 4):512 * (c - 4) + 512]
            return PS_C.ap()[:, 512 * (c - 6):512 * (c - 6) + 512]

        v_asl, v_hkmm = {}, {}
        for k in range(KD):
            if k >= 2:
                wt("po", s_pe, v_hkmm.get(k - 2, 0))
            v_asl[k] = dma("po", dma_p,
                           BWS.ap()[:, 512 * (k % 2):512 * (k % 2) + 512],
                           ap(aall, 512 * k, [[8192, 128], [1, 512]]))
            wt("pe", dma_p, v_asl[k])
            if k == 0:
                wt("pe", s_ac, vQ_swpevac[15])
            for c in range(NCH):
                op("pe", "matmul", h_psum(c),
                   ap(bcC, 1024 * k + 128 * c, [[16384, 128], [1, 128]]),
                   BWS.ap()[:, 512 * (k % 2):512 * (k % 2) + 512],
                   start=(k == 0), stop=(k == KD - 1),
                   inc=(s_pe if c == NCH - 1 else None), **MM)
            v_hkmm[k] = s_pe.n
        v_hmmall = v_hkmm[KD - 1]
        v_hevac2, v_hTmm2, v_hTcp = {}, {}, {}
        corder = [6, 7, 0, 1, 2, 3, 4, 5]
        wt("dv", s_pe, v_hmmall)
        for idx, c in enumerate(corder):
            if idx >= 2:
                wt("dv", s_pe, v_hTmm2[corder[idx - 2]])
            for ee in range(8):
                op("dv", "tensor_scalar_mul",
                   SCM.ap()[:, 512 * (idx % 2) + 64 * ee:512 * (idx % 2) + 64 * ee + 64],
                   bass.AP(h_psum(c).tensor, h_psum(c).offset + 64 * ee,
                           [[h_psum(c).tensor.shape[-1], 128], [1, 64]]),
                   GW.ap()[:, 432 + 8 * c + ee:433 + 8 * c + ee],
                   inc=(s_dv if ee == 7 else None))
            v_hevac2[c] = s_dv.n
            wt("pe", s_dv, v_hevac2[c])
            if idx >= 1:
                wt("pe", s_dv, v_hTcp[corder[idx - 1]])
            for b in range(4):
                op("pe", "transpose", ap(bcPC, 128 * b, [[2048, 128], [1, 128]]),
                   SCM.ap()[:, 512 * (idx % 2) + 128 * b:512 * (idx % 2) + 128 * b + 128],
                   IDENTB.ap(), inc=(s_pe if b == 3 else None))
            v_hTmm2[c] = s_pe.n
            wt("dv", s_pe, v_hTmm2[c])
            for b in range(4):
                op("dv", "tensor_copy",
                   HT.ap()[:, 1024 * b + 128 * c:1024 * b + 128 * c + 128],
                   ap(bcPC, 128 * b, [[2048, 128], [1, 128]]),
                   inc=(s_dv if b == 3 else None))
            v_hTcp[c] = s_dv.n
        if "hT" in dbg:
            wt("sy", s_dv, v_hTcp[corder[-1]])
            tap("hT", HT.ap())

        v_wslw, v_bwsw = {}, {}
        wt("po", s_pe, v_hmmall)
        v_wslw[0] = dma("po", dma_p, WSL.ap()[:, 0:2048],
                        ap(wo, 0, [[2048, 128], [1, 2048]]))
        v_bwsw[0] = dma("po", dma_p, BWS.ap()[:, 0:512],
                        ap(bwm, 0, [[512, 128], [1, 512]]))
        wt("po", s_pe, vQ_pmm[15])
        v_wslw[1] = dma("po", dma_p, WSL.ap()[:, 2048:4096],
                        ap(wo, 128 * 2048, [[2048, 128], [1, 2048]]))
        v_bwsw[1] = dma("po", dma_p, BWS.ap()[:, 512:1024],
                        ap(bwm, 128 * 512, [[512, 128], [1, 512]]))

        kvin_bf = ap(bcS, 4096, [[8192, 128], [1, 2048]])
        v_gth, v_comb = {}, {}
        wt("sy", s_cc, v_ccc)
        for i in range(NCORES):
            if i >= 1:
                wt("sy", s_dv, v_comb[i - 1])
            v_gth[i] = dma("sy", dma_s, STBF.ap(),
                           ap(cc_out, i * 128 * 2048, [[2048, 128], [1, 2048]]))
            wt("dv", dma_s, v_gth[i])
            for h in range(H):
                gsl = STBF.ap()[:, 128 * h:128 * h + 128]
                ksl = ap(bcS, 4096 + 128 * h, [[8192, 128], [1, 128]])
                co = COEFC.ap()[:, 16 * i + h:16 * i + h + 1]
                last = (s_dv if h == H - 1 else None)
                if i == 0:
                    op("dv", "tensor_scalar_mul", ksl, gsl, co, inc=last)
                else:
                    op("dv", "scalar_tensor_tensor", ksl, gsl, co, ksl,
                       ALU.mult, ALU.add, inc=last)
            v_comb[i] = s_dv.n
        if "KVin" in dbg:
            wt("sy", s_dv, v_comb[NCORES - 1])
            tap("KVin", kvin_bf)

        v_kdt2, v_kdp = {}, {}

        def emit_kd_dma(c):
            if c < 2:
                wt("sy", s_pe, v_hmmall)
            vv = 0
            for h in range(H):
                vv = dma("sy", dma_s,
                         ap(bcC, 2048 * c + 128 * h, [[16384, 128], [1, 128]]),
                         ap(bcA, KT0 + 1024 * h + 128 * c, [[32768, 128], [1, 128]]),
                         transpose=True)
            v_kdt2[c] = vv

        def emit_kdp(c):
            wt("dv", dma_s, v_kdt2[c])
            for h in range(H):
                sl = ap(bcC, 2048 * c + 128 * h, [[16384, 128], [1, 128]])
                op("dv", "tensor_scalar_mul", sl, sl, KDECC.ap()[:, h:h + 1],
                   inc=(s_dv if h == H - 1 else None))
            v_kdp[c] = s_dv.n

        emit_kd_dma(0)
        emit_kd_dma(1)
        emit_kdp(0)
        emit_kdp(1)
        barrier()

        attn_sc = SCR.ap()[:, 0:2048]
        v_qd, v_stbf2, v_scores, v_maskevac, v_outmm = {}, {}, {}, {}, {}
        v_outevac, v_p2state, v_sqg, v_sqmm, v_rmsrow = {}, {}, {}, {}, {}
        v_rstdbcmm, v_rstdsb, v_rmsg = {}, {}, {}
        for c in range(NCH):
            if c + 2 < NCH:
                emit_kd_dma(c + 2)
            if c > 0:
                wt("dv", s_pe, v_p2state[c - 1])
            v_qd[c] = op("dv", "tensor_mul",
                         ap(QD, 0, [[2048, 128], [128, 16], [1, 128]]),
                         ap(bcA, 128 * c, [[32768, 128], [1024, 16], [1, 128]]),
                         ap(QDECC, 0, [[2048, 128], [128, 16], [1, 128]]),
                         inc=s_dv)
            if c > 0:
                wt("ac", s_pe, v_p2state[c - 1])
                v_stbf2[c] = op("ac", "copy", STBF.ap(), PS_A.ap()[:, 0:2048], inc=s_ac)

            def emit_scores(G, c=c):
                gn = 4 * c + G
                if gn >= 2:
                    wt("pe", s_dv, v_maskevac[gn - 2])
                for h in range(4 * G, 4 * G + 4):
                    sl = PS_B.ap()[:, 512 * (gn % 2) + 128 * (h % 4):
                                   512 * (gn % 2) + 128 * (h % 4) + 128]
                    op("pe", "matmul", sl,
                       ap(bcA, KT0 + 1024 * h + 128 * c, [[32768, 128], [1, 128]]),
                       ap(bcA, 1024 * h + 128 * c, [[32768, 128], [1, 128]]),
                       start=True, stop=True, inc=(s_pe if h % 4 == 3 else None), **MM)
                v_scores[gn] = s_pe.n

            def emit_mask(G, c=c):
                gn = 4 * c + G
                wt("dv", s_pe, v_scores[gn])
                prevkey = (c, G - 2) if G >= 2 else (c - 1, G + 2)
                if prevkey in v_outmm:
                    wt("dv", s_pe, v_outmm[prevkey])
                op("dv", "tensor_mul", SCM.ap()[:, 512 * (gn % 2):512 * (gn % 2) + 512],
                   PS_B.ap()[:, 512 * (gn % 2):512 * (gn % 2) + 512],
                   DMASKC.ap()[:, 512 * G:512 * G + 512], inc=s_dv)
                v_maskevac[gn] = s_dv.n

            def emit_out(G, c=c):
                gn = 4 * c + G
                wt("pe", s_dv, v_maskevac[gn])
                if G == 0:
                    wt("pe", s_dv, v_qd[c])
                    if c > 0:
                        wt("pe", s_ac, v_stbf2[c])
                        wt("pe", s_ac, v_rmsrow[c - 1])
                    else:
                        wt("pe", s_dv, v_comb[NCORES - 1])
                if G == 1 and c > 0:
                    wt("pe", s_ac, v_rstdsb[c - 1])
                prevkey = (c, G - 2) if G >= 2 else (c - 1, G + 2)
                if prevkey in v_outevac:
                    wt("pe", s_ac, v_outevac[prevkey])
                for h in range(4 * G, 4 * G + 4):
                    sl = PS_C.ap()[:, 512 * (gn % 2) + 128 * (h % 4):
                                   512 * (gn % 2) + 128 * (h % 4) + 128]
                    op("pe", "matmul", sl,
                       ap(bcB, 2048 * c + 128 * h, [[16384, 128], [1, 128]]),
                       SCM.ap()[:, 512 * (gn % 2) + 128 * (h % 4):
                                512 * (gn % 2) + 128 * (h % 4) + 128],
                       start=True, stop=False, **MM)
                    if c == 0:
                        stsl = ap(bcS, 4096 + 128 * h, [[8192, 128], [1, 128]])
                    else:
                        stsl = STBF.ap()[:, 128 * h:128 * h + 128]
                    op("pe", "matmul", sl, stsl, QD.ap()[:, 128 * h:128 * h + 128],
                       start=False, stop=True, inc=(s_pe if h % 4 == 3 else None), **MM)
                v_outmm[(c, G)] = s_pe.n

            def emit_state2(G, c=c):
                if c == NCH - 1:
                    return
                if G == 0:
                    wt("pe", s_dv, v_kdp[c])
                for h in range(4 * G, 4 * G + 4):
                    st_out = PS_A.ap()[:, 128 * h:128 * h + 128]
                    if c == 0:
                        ssl = ap(bcS, 4096 + 128 * h, [[8192, 128], [1, 128]])
                    else:
                        ssl = STBF.ap()[:, 128 * h:128 * h + 128]
                    op("pe", "matmul", st_out, RCIC.ap()[:, 128 * h:128 * h + 128],
                       ssl, start=True, stop=False, **MM)
                    op("pe", "matmul", st_out,
                       ap(bcC, 2048 * c + 128 * h, [[16384, 128], [1, 128]]),
                       ap(bcB, 2048 * c + 128 * h, [[16384, 128], [1, 128]]),
                       start=False, stop=True,
                       inc=(s_pe if G == 3 and h % 4 == 3 else None), **MM)
                if G == 3:
                    v_p2state[c] = s_pe.n

            def emit_outevac(G, c=c):
                gn = 4 * c + G
                wt("ac", s_pe, v_outmm[(c, G)])
                if c > 0:
                    wt("ac", s_dv, v_rmsg[(c - 1, G)])
                v_outevac[(c, G)] = op("ac", "copy",
                                       SCR.ap()[:, 512 * G:512 * G + 512],
                                       PS_C.ap()[:, 512 * (gn % 2):512 * (gn % 2) + 512],
                                       inc=s_ac)
                v_sqg[(c, G)] = op("ac", "activation",
                                   ap(bcS, 6144 + 512 * G, [[8192, 128], [1, 512]]),
                                   SCR.ap()[:, 512 * G:512 * G + 512], AF.Square,
                                   inc=s_ac)

            def emit_sqmm(G, c=c):
                if G == 0:
                    wt("pe", s_ac, v_sqg[(c, 2)])
                wt("pe", s_ac, v_sqg[(c, G)])
                for h in range(4 * G, 4 * G + 4):
                    op("pe", "matmul", PS_C.ap()[0:1, 0:128], ONESC.ap(),
                       ap(bcS, 6144 + 128 * h, [[8192, 128], [1, 128]]),
                       start=(h == 0), stop=(h == H - 1),
                       inc=(s_pe if h == H - 1 else None), **MM)
                if G == 3:
                    v_sqmm[c] = s_pe.n

            def emit_rms(c2, Gs, c=c):
                if Gs[0] == 0:
                    wt("dv", s_ac, v_rstdsb[c2])
                    wt("dv", s_pe,
                       v_p2state[c2] if c2 < NCH - 1 else v_outmm[(c2, 3)])
                for G in Gs:
                    op("dv", "tensor_mul",
                       ap(SCR, 512 * G, [[4096, 128], [128, 4], [1, 128]]),
                       ap(SCR, 512 * G, [[4096, 128], [128, 4], [1, 128]]),
                       ap(VPX, 0, [[4096, 128], [0, 4], [1, 128]]))
                    v_rmsg[(c2, G)] = op("dv", "tensor_mul",
                                         ap(bcB, 2048 * c2 + 512 * G,
                                            [[16384, 128], [1, 128], [128, 4]]),
                                         ap(SCR, 512 * G,
                                            [[4096, 128], [1, 128], [128, 4]]),
                                         ap(RMSC, 4 * G, [[16, 128], [0, 128], [1, 4]]),
                                         inc=s_dv)

            emit_scores(0)
            emit_mask(0)
            emit_scores(1)
            emit_out(0)
            emit_mask(1)
            if c >= 1:
                emit_rms(c - 1, [0, 1])
            emit_outevac(0)
            emit_scores(2)
            emit_out(1)
            emit_state2(0)
            emit_mask(2)
            emit_outevac(1)
            emit_scores(3)
            emit_out(2)
            emit_state2(1)
            emit_mask(3)
            if c >= 1:
                emit_rms(c - 1, [2, 3])
            if c + 2 < NCH:
                emit_kdp(c + 2)
            emit_outevac(2)
            emit_out(3)
            emit_state2(2)
            emit_sqmm(0)
            emit_sqmm(1)
            emit_sqmm(2)
            emit_state2(3)
            emit_outevac(3)
            emit_sqmm(3)
            wt("ac", s_pe, v_sqmm[c])
            op("ac", "activation", GW.ap()[0:1, 16:144], PS_C.ap()[0:1, 0:128],
               AF.Ln, bias=EPSC.ap()[0:1, :], scale=1.0 / D)
            v_rmsrow[c] = op("ac", "activation", GW.ap()[0:1, 144:272],
                             GW.ap()[0:1, 16:144], AF.Exp, scale=-0.5, inc=s_ac)
            wt("pe", s_ac, v_rmsrow[c])
            op("pe", "matmul", PS_C.ap()[:, 512:640], GW.ap()[0:1, 304:432],
               GW.ap()[0:1, 144:272], start=True, stop=True, inc=s_pe, **MM)
            v_rstdbcmm[c] = s_pe.n
            wt("ac", s_pe, v_rstdbcmm[c])
            v_rstdsb[c] = op("ac", "copy", VPX.ap()[:, 0:128], PS_C.ap()[:, 512:640],
                             inc=s_ac)
            last_rms = emit_rms
        last_rms(NCH - 1, [0, 1, 2, 3])
        barrier()
        if "attn" in dbg:
            tap("attn", ap(bcB, 0, [[16384, 128], [1, 16384]]))
        if "kd" in dbg:
            tap("kd", ap(bcC, 0, [[16384, 128], [1, 16384]]))

        v_wmm, v_wevac, v_xsc, v_outst = {}, {}, {}, {}
        for m in range(16):
            if m >= 2:
                wt("po", s_pe, v_wmm[m - 2])
                v_wslw[m] = dma("po", dma_p,
                                WSL.ap()[:, 2048 * (m % 2):2048 * (m % 2) + 2048],
                                ap(wo, m * 128 * 2048, [[2048, 128], [1, 2048]]))
                v_bwsw[m] = dma("po", dma_p,
                                BWS.ap()[:, 512 * (m % 2):512 * (m % 2) + 512],
                                ap(bwm, m * 128 * 512, [[512, 128], [1, 512]]))
            if m >= 2:
                wt("sy", s_dv, v_wevac[m - 2])
            v_xsc[m] = dma("sy", dma_s,
                           VPX.ap()[:, 1024 * (m % 2):1024 * (m % 2) + 1024],
                           ap(xt, m * 128 * 1024, [[1024, 128], [1, 1024]]))
            wt("pe", dma_p, v_wslw[m])
            wt("pe", dma_p, v_bwsw[m])
            if m >= 2:
                wt("pe", s_dv, v_wevac[m - 2])
            for hf in range(2):
                pout = PS_A.ap()[:, 1024 * (m % 2) + 512 * hf:
                                 1024 * (m % 2) + 512 * hf + 512]
                for k in range(KD):
                    op("pe", "matmul", pout,
                       WSL.ap()[:, 2048 * (m % 2) + 128 * k:
                                2048 * (m % 2) + 128 * k + 128],
                       ap(bcB, 8192 * hf + 128 * k, [[16384, 128], [2048, 4], [1, 128]]),
                       start=(k == 0), stop=False, **MM)
                for k4 in range(4):
                    op("pe", "matmul", pout,
                       BWS.ap()[:, 512 * (m % 2) + 128 * k4:
                                512 * (m % 2) + 128 * k4 + 128],
                       HT.ap()[:, 1024 * k4 + 512 * hf:1024 * k4 + 512 * hf + 512],
                       start=False, stop=(k4 == 3),
                       inc=(s_pe if hf == 1 and k4 == 3 else None), **MM)
            v_wmm[m] = s_pe.n
            wt("dv", s_pe, v_wmm[m])
            wt("dv", dma_s, v_xsc[m])
            if m >= 2:
                wt("dv", dma_s, v_outst[m - 2])
            v_wevac[m] = op("dv", "tensor_add",
                            VPX.ap()[:, 2048 + 1024 * (m % 2):2048 + 1024 * (m % 2) + 1024],
                            PS_A.ap()[:, 1024 * (m % 2):1024 * (m % 2) + 1024],
                            VPX.ap()[:, 1024 * (m % 2):1024 * (m % 2) + 1024],
                            inc=s_dv)
            wt("sy", s_dv, v_wevac[m])
            v_outst[m] = dma("sy", dma_s,
                             ap(out_t, m * 128 * 1024, [[1024, 128], [1, 1024]]),
                             VPX.ap()[:, 2048 + 1024 * (m % 2):
                                      2048 + 1024 * (m % 2) + 1024])
        wt("sy", dma_s, v_outst[15])

        block = es.enter_context(nc.Block())

        @block.tensor
        def _(e):
            for f in ST["pe"]:
                f(e)

        @block.vector
        def _(e):
            for f in ST["dv"]:
                f(e)

        @block.scalar
        def _(e):
            for f in ST["ac"]:
                f(e)

        @block.gpsimd
        def _(e):
            for f in ST["po"]:
                f(e)

        @block.sync
        def _(e):
            for f in ST["sy"]:
                f(e)

    return nc


_CACHE = {}


def _get_nc():
    if "nc" not in _CACHE:
        _CACHE["nc"] = build_nc(())
    return _CACHE["nc"]


def kernel(L0_attn_x, L0_attn_Wq, L0_attn_Wk, L0_attn_Wv, L0_attn_Wo,
           L0_attn_rms_w, L0_ffn_A, L0_ffn_Bw, L0_ffn_Wg, _taps=None):
    x = np.asarray(L0_attn_x)
    wts = _host_weights(np.asarray(L0_attn_Wq), np.asarray(L0_attn_Wk),
                        np.asarray(L0_attn_Wv), np.asarray(L0_attn_Wo),
                        np.asarray(L0_attn_rms_w), np.asarray(L0_ffn_A),
                        np.asarray(L0_ffn_Bw), np.asarray(L0_ffn_Wg))
    consts = _host_constants()
    nc = build_nc(_taps) if _taps else _get_nc()
    in_maps = []
    for j in range(NCORES):
        m = {"xt": np.ascontiguousarray(x[0, j * NTOK:(j + 1) * NTOK, :].T)}
        m.update(wts)
        m.update(consts)
        m.update(_host_core_constants(j))
        in_maps.append(m)
    res = run_bass_kernel_spmd(nc, in_maps, core_ids=list(range(NCORES)))
    out = np.concatenate([res.results[j]["out"].T for j in range(NCORES)], axis=0)
    out = out.reshape(1, 8192, 2048).astype(np.float32)
    if _taps:
        return out, res
    return out


# version 11
# speedup vs baseline: 1.2953x; 1.0741x over previous
import contextlib

import ml_dtypes
import numpy as np

import concourse.bass as bass
import concourse.mybir as mybir
from concourse.bass_utils import run_bass_kernel_spmd

F32 = mybir.dt.float32
BF16 = mybir.dt.bfloat16
AF = mybir.ActivationFunctionType
ALU = mybir.AluOpType
AX = mybir.AxisListType
BFNP = ml_dtypes.bfloat16

NCORES = 8
D = 2048
H = 16
CH = 128
NCH = 8
NTOK = 1024
KD = 16
EPS = 1e-5


def _decay_rates():
    i = np.arange(H, dtype=np.float64)
    slopes = 2.0 ** (-(i + 1) / 2.0)
    return np.exp(-slopes)


def _host_constants():
    r = _decay_rates()
    c = np.arange(CH, dtype=np.float64)
    diff = c[None, :] - c[:, None]
    dm = np.where(diff >= 0, r[:, None, None] ** np.maximum(diff, 0.0)[None], 0.0)
    dmask = np.transpose(dm, (1, 0, 2)).reshape(CH, H * CH)
    qd = (r[:, None] ** (c[None, :] + 1.0))
    qdec = np.broadcast_to(qd.reshape(1, H * CH), (CH, H * CH))
    rci = np.zeros((CH, H * CH))
    for h in range(H):
        rci[:, h * CH:(h + 1) * CH] = np.eye(CH) * (r[h] ** CH)
    kdec = (r[None, :] ** (CH - 1.0 - c[:, None]))
    sw = np.zeros((CH, CH))
    for k in range(CH):
        sw[k, (k + 64) % CH] = 1.0
    return {
        "dmask": dmask.astype(BFNP),
        "qdec": np.ascontiguousarray(qdec).astype(BFNP),
        "rci": rci.astype(BFNP),
        "kdecc": kdec.astype(np.float32),
        "swapm": sw.astype(BFNP),
        "identb": np.eye(CH).astype(BFNP),
    }


def _host_core_constants(core):
    r = _decay_rates()
    t = np.arange(NTOK, dtype=np.float64) + core * NTOK
    i = np.arange(32, dtype=np.float64)
    freq = (1.0 / 1024.0) ** (i / 31.0)
    theta = t[None, :] * freq[:, None]
    cosr = np.ones((CH, NTOK))
    sinr = np.zeros((CH, NTOK))
    cosr[0:32] = np.cos(theta)
    cosr[64:96] = np.cos(theta)
    sinr[0:32] = np.sin(theta)
    sinr[64:96] = -np.sin(theta)
    dec8 = r ** (CH * NCH)
    coef = np.zeros((CH, CH))
    for ii in range(NCORES):
        if ii < core:
            coef[:, 16 * ii:16 * ii + 16] = dec8[None, :] ** (core - 1 - ii)
    return {
        "cosr": cosr.astype(BFNP),
        "sinr": sinr.astype(BFNP),
        "coefc": coef.astype(np.float32),
    }


def _host_weights(Wq, Wk, Wv, Wo, rms_w, A, Bw, Wg):
    def m_major(W):
        return np.ascontiguousarray(
            W.reshape(16, 128, 16, 128).transpose(2, 1, 0, 3)).reshape(16, 128, 2048)
    wv_r = np.ascontiguousarray(
        Wv.reshape(16, 128, 4, 512).transpose(2, 1, 0, 3)).reshape(4, 128, 8192)
    a_r = np.ascontiguousarray(
        A.transpose(2, 0, 1).reshape(2048, 512).reshape(16, 128, 512)
        .transpose(1, 0, 2)).reshape(128, 8192)
    bwt = Bw.transpose(0, 2, 1).reshape(512, 2048)
    bw_r = np.ascontiguousarray(
        bwt.reshape(4, 128, 16, 128).transpose(2, 1, 0, 3)).reshape(16, 128, 512)
    wg_r = np.ascontiguousarray(
        Wg.T.reshape(16, 128, 8).transpose(1, 0, 2)).reshape(128, 128)
    rms_r = np.ascontiguousarray(rms_w.reshape(16, 128).T)
    return {
        "wq": m_major(Wq), "wk": m_major(Wk), "wo": m_major(Wo),
        "wv": wv_r, "aall": a_r, "bwm": bw_r, "wg": wg_r, "rmsw": rms_r,
    }


class Sem:
    def __init__(self, h):
        self.h = h
        self.n = 0

    def bump(self, k=1):
        self.n += k
        return self.n


def build_nc(taps=()):
    nc = bass.Bass("TRN2", target_bir_lowering=False,
                   dynamic_dma_scratch_size=12288)

    def din(name, shape, dt):
        return nc.dram_tensor(name, shape, dt, kind="ExternalInput")

    xt = din("xt", [D, NTOK], F32)
    wq = din("wq", [16, 128, 2048], F32)
    wk = din("wk", [16, 128, 2048], F32)
    wo = din("wo", [16, 128, 2048], F32)
    wv = din("wv", [4, 128, 8192], F32)
    aall = din("aall", [128, 8192], F32)
    bwm = din("bwm", [16, 128, 512], F32)
    wg = din("wg", [128, 128], F32)
    rmsw = din("rmsw", [128, 16], F32)
    dmaskd = din("dmask", [128, 2048], BF16)
    qdecd = din("qdec", [128, 2048], BF16)
    rcid = din("rci", [128, 2048], BF16)
    kdeccd = din("kdecc", [128, 16], F32)
    coefcd = din("coefc", [128, 128], F32)
    cosrd = din("cosr", [128, 1024], BF16)
    sinrd = din("sinr", [128, 1024], BF16)
    swapmd = din("swapm", [128, 128], BF16)
    identbd = din("identb", [128, 128], BF16)
    out_t = nc.dram_tensor("out", [D, NTOK], F32, kind="ExternalOutput")
    cc_in = nc.dram_tensor("cc_in", [128, 2048], BF16)
    cc_out = nc.dram_tensor("cc_out", [1024, 2048], BF16, addr_space="Shared")
    dbg = {}
    for tp in taps:
        shp, dt = taps[tp]
        dbg[tp] = nc.dram_tensor("dbg_" + tp, shp, dt, kind="ExternalOutput")

    def sb(name, shape, dt):
        return nc.alloc_sbuf_tensor(name, shape, dt)

    BUF_A = sb("buf_a", [128, 16384], F32)
    BUF_B = sb("buf_b", [128, 8192], F32)
    BUF_C = sb("buf_c", [128, 8192], F32)
    HT = sb("ht", [128, 4096], BF16)
    VPX = sb("vpx", [128, 4096], F32)
    SCR = sb("scr", [128, 4096], F32)
    WSL = sb("wsl", [128, 4096], BF16)
    BWS = sb("bws", [128, 1024], BF16)
    QD = sb("qd", [128, 2048], BF16)
    SCM = sb("scm", [128, 1024], BF16)
    STBF = sb("stbf", [128, 2048], BF16)
    DMASKC = sb("dmaskc", [128, 2048], BF16)
    QDECC = sb("qdecc", [128, 2048], BF16)
    RCIC = sb("rcic", [128, 2048], BF16)
    COSR = sb("cosrc", [128, 1024], BF16)
    SINR = sb("sinrc", [128, 1024], BF16)
    KDECC = sb("kdeccc", [128, 16], F32)
    RMSC = sb("rmsc", [128, 16], F32)
    COEFC = sb("coefcc", [128, 128], F32)
    WGT = sb("wgt", [128, 128], BF16)
    IDENTB = sb("identbc", [128, 128], BF16)
    SWAPM = sb("swapmc", [128, 128], BF16)
    ONESC = sb("onesc", [128, 1], BF16)
    EPSC = sb("epsc", [128, 1], F32)
    GW = sb("gw", [128, 496], F32)
    GEXP = sb("gexp", [128, 8], F32)

    PS_A = nc.alloc_psum_tensor("ps_a", [128, 2048], F32)
    PS_B = nc.alloc_psum_tensor("ps_b", [128, 1024], F32)
    PS_C = nc.alloc_psum_tensor("ps_c", [128, 1024], F32)

    bcA = BUF_A.ap().bitcast(BF16).tensor
    bcB = BUF_B.ap().bitcast(BF16).tensor
    bcC = BUF_C.ap().bitcast(BF16).tensor
    bcV = VPX.ap().bitcast(BF16).tensor
    bcS = SCR.ap().bitcast(BF16).tensor
    bcPB = PS_B.ap().bitcast(BF16).tensor
    bcPC = PS_C.ap().bitcast(BF16).tensor
    KT0 = 16384
    bcSTBF = STBF.ap().bitcast(F32).tensor
    bcSCM = SCM.ap().bitcast(F32).tensor

    def ap(t, col, dims, prow=0):
        pitch = t.shape[-1]
        return bass.AP(t, prow * pitch + col, dims)

    ST = {"pe": [], "dv": [], "ac": [], "po": [], "sy": []}

    def em(k, f):
        ST[k].append(f)

    with contextlib.ExitStack() as es:
        s_pe = Sem(es.enter_context(nc.semaphore("s_pe")))
        s_dv = Sem(es.enter_context(nc.semaphore("s_dv")))
        s_ac = Sem(es.enter_context(nc.semaphore("s_ac")))
        s_po = Sem(es.enter_context(nc.semaphore("s_po")))
        s_cc = Sem(es.enter_context(nc.semaphore("s_cc")))
        dma_p = Sem(es.enter_context(nc.semaphore("dma_p")))
        dma_s = Sem(es.enter_context(nc.semaphore("dma_s")))
        dma_a = Sem(es.enter_context(nc.semaphore("dma_a")))
        ALLSEMS = [s_pe, s_dv, s_ac, s_po, s_cc, dma_p, dma_s, dma_a]
        MYSEM = {"pe": s_pe, "dv": s_dv, "ac": s_ac, "po": s_po, "sy": None}

        def wt(k, sem, v):
            if v and v > 0:
                em(k, lambda e, s=sem.h, vv=v: e.wait_ge(s, vv))

        def op(k, name, *a, inc=None, **kw):
            v = inc.bump(1) if inc is not None else None
            def f(e, name=name, a=a, kw=kw, inc=inc):
                ins = getattr(e, name)(*a, **kw)
                if inc is not None:
                    ins.then_inc(inc.h, 1)
            em(k, f)
            return v

        def dma(k, sem, out_ap, in_ap, transpose=False):
            v = sem.bump(16)
            em(k, lambda e, o=out_ap, i=in_ap, s=sem.h, tr=transpose:
               e.dma_start(out=o, in_=i, transpose=tr).then_inc(s, 16))
            return v

        def barrier():
            snap = [(s, s.n) for s in ALLSEMS]
            for k in ST:
                own = MYSEM[k]
                for s, v in snap:
                    if s is not own and v > 0:
                        wt(k, s, v)

        def tap(name, src_ap):
            if name in dbg:
                dma("sy", dma_s, dbg[name].ap(), src_ap)

        MM = dict(skip_group_check=True)

        for dst, src in [(DMASKC, dmaskd), (QDECC, qdecd), (RCIC, rcid),
                         (COSR, cosrd), (SINR, sinrd), (KDECC, kdeccd),
                         (RMSC, rmsw), (COEFC, coefcd), (IDENTB, identbd),
                         (SWAPM, swapmd)]:
            dma("sy", dma_s, dst.ap(), src.ap())
        v_syconsts = dma_s.n
        op("dv", "memset", ONESC.ap(), 1.0)
        op("dv", "memset", GW.ap()[0:1, 304:432], 1.0)
        v_ms0 = op("dv", "memset", EPSC.ap(), EPS, inc=s_dv)
        v_xbt = []
        for kd in range(KD):
            v_xbt.append(dma("po", dma_p,
                             ap(bcB, 1024 * kd, [[16384, 128], [1, 1024]]),
                             ap(xt, kd * 128 * 1024, [[1024, 128], [1, 1024]])))
        v_wg = dma("po", dma_p, WGT.ap(), wg.ap())
        wt("pe", s_dv, v_ms0)
        wt("ac", s_dv, v_ms0)

        v_sq1, v_sqmm1 = [], []
        for kd in range(KD):
            wt("pe", dma_p, v_xbt[kd])
            for hf in range(2):
                op("pe", "matmul", PS_B.ap()[0:1, 512 * hf:512 * hf + 512],
                   ONESC.ap(), ap(bcB, 1024 * kd + 512 * hf, [[16384, 128], [1, 512]]),
                   start=(kd == 0), stop=(kd == KD - 1),
                   inc=(s_pe if kd == KD - 1 and hf == 1 else None), **MM)
        for kd in range(KD):
            if kd >= 2:
                wt("ac", s_pe, v_sqmm1[kd - 2])
            wt("ac", dma_p, v_xbt[kd])
            v_sq1.append(op("ac", "activation",
                            QD.ap()[:, 1024 * (kd % 2):1024 * (kd % 2) + 1024],
                            ap(bcB, 1024 * kd, [[16384, 128], [1, 1024]]),
                            AF.Square, inc=s_ac))
            wt("pe", s_ac, v_sq1[kd])
            for hf in range(2):
                op("pe", "matmul", PS_C.ap()[0:1, 512 * hf:512 * hf + 512],
                   ONESC.ap(),
                   QD.ap()[:, 1024 * (kd % 2) + 512 * hf:1024 * (kd % 2) + 512 * hf + 512],
                   start=(kd == 0), stop=(kd == KD - 1),
                   inc=(s_pe if hf == 1 else None), **MM)
            v_sqmm1.append(s_pe.n)
        v_stats = v_sqmm1[-1]

        row_mean = SCR.ap()[0:1, 1024:2048]
        row_rstd = SCR.ap()[0:1, 2048:3072]
        row_mrs = SCR.ap()[0:1, 3072:4096]
        row_tmp = SCR.ap()[0:1, 0:1024]
        wt("dv", s_pe, v_stats)
        op("dv", "tensor_scalar_mul", row_mean, PS_B.ap()[0:1, 0:1024], 1.0 / D)
        op("dv", "drain")
        op("dv", "tensor_mul", row_tmp, row_mean, row_mean)
        op("dv", "tensor_scalar_mul", row_rstd, PS_C.ap()[0:1, 0:1024], 1.0 / D)
        op("dv", "drain")
        v_var = op("dv", "tensor_sub", row_rstd, row_rstd, row_tmp, inc=s_dv)
        wt("ac", s_dv, v_var)
        op("ac", "activation", row_tmp, row_rstd, AF.Ln, bias=EPSC.ap()[0:1, :], scale=1.0)
        v_rstd = op("ac", "activation", row_rstd, row_tmp, AF.Exp, scale=-0.5, inc=s_ac)
        wt("dv", s_ac, v_rstd)
        v_mrs = op("dv", "tensor_mul", row_mrs, row_mean, row_rstd, inc=s_dv)
        wt("pe", s_dv, v_mrs)
        for hf in range(2):
            op("pe", "matmul", PS_B.ap()[:, 512 * hf:512 * hf + 512],
               GW.ap()[0:1, 304:432],
               SCR.ap()[0:1, 2048 + 512 * hf:2048 + 512 * hf + 512],
               start=True, stop=True, **MM)
            op("pe", "matmul", PS_C.ap()[:, 512 * hf:512 * hf + 512],
               GW.ap()[0:1, 304:432],
               SCR.ap()[0:1, 3072 + 512 * hf:3072 + 512 * hf + 512],
               start=True, stop=True, inc=(s_pe if hf == 1 else None), **MM)
        v_bcast = s_pe.n

        rstd_bc = ap(bcS, 0, [[8192, 128], [1, 1024]])
        mrs_bc = ap(bcS, 1024, [[8192, 128], [1, 1024]])
        wt("ac", s_pe, v_bcast)
        op("ac", "copy", rstd_bc, PS_B.ap()[:, 0:1024])
        v_bcsb = op("ac", "copy", mrs_bc, PS_C.ap()[:, 0:1024], inc=s_ac)
        xnf = ap(bcS, 2048, [[8192, 128], [1, 1024]])
        v_xn = []
        wt("dv", s_ac, v_bcsb)
        for kd in range(KD):
            op("dv", "tensor_mul", xnf,
               ap(bcB, 1024 * kd, [[16384, 128], [1, 1024]]), rstd_bc)
            v_xn.append(op("dv", "tensor_sub",
                           ap(bcC, 1024 * kd, [[16384, 128], [1, 1024]]),
                           xnf, mrs_bc, inc=s_dv))
        wt("pe", dma_p, v_wg)
        for c in range(NCH):
            for kd in range(KD):
                if c == 0:
                    wt("pe", s_dv, v_xn[kd])
                op("pe", "matmul", PS_C.ap()[:, 8 * c:8 * c + 8],
                   ap(bcC, 1024 * kd + 128 * c, [[16384, 128], [1, 128]]),
                   WGT.ap()[:, 8 * kd:8 * kd + 8],
                   start=(kd == 0), stop=(kd == KD - 1),
                   inc=(s_pe if c == NCH - 1 and kd == KD - 1 else None), **MM)
        v_logitsmm = s_pe.n
        if "xn" in dbg:
            wt("sy", s_dv, v_xn[-1])
            tap("xn", ap(bcC, 0, [[16384, 128], [1, 16384]]))

        wt("dv", s_pe, v_logitsmm)
        v_lgtm = op("dv", "tensor_copy", ap(bcSCM, 0, [[512, 128], [1, 64]]),
                    PS_C.ap()[:, 0:64], inc=s_dv)
        for c in range(NCH):
            lg = ap(bcSCM, 8 * c, [[512, 128], [1, 8]])
            op("dv", "reduce_max", GW.ap()[:, 0:1], lg, AX.X)
            op("dv", "drain")
            v_g1 = op("dv", "tensor_scalar_mul", GW.ap()[:, 1:2], GW.ap()[:, 0:1],
                      -1.0, inc=s_dv)
            wt("ac", s_dv, v_g1)
            v_g2c = op("ac", "activation", GEXP.ap(), lg, AF.Exp,
                       bias=GW.ap()[:, 1:2], scale=1.0, inc=s_ac)
            wt("dv", s_ac, v_g2c)
            op("dv", "reduce_sum", GW.ap()[:, 2:3], GEXP.ap(), AX.X)
            op("dv", "drain")
            op("dv", "reciprocal", GW.ap()[:, 3:4], GW.ap()[:, 2:3])
            op("dv", "drain")
            op("dv", "tensor_scalar_mul", GW.ap()[:, 2:3], GW.ap()[:, 3:4], 4.0)
            op("dv", "tensor_single_scalar", GEXP.ap(), lg, GW.ap()[:, 0:1], ALU.is_equal)
            op("dv", "drain")
            op("dv", "tensor_scalar_mul", GW.ap()[:, 432 + 8 * c:440 + 8 * c],
               GEXP.ap(), GW.ap()[:, 2:3], inc=(s_dv if c == NCH - 1 else None))
        v_sgate = s_dv.n
        if "sgate" in dbg:
            wt("sy", s_dv, v_sgate)
            tap("sgate", GW.ap()[:, 432:496])
            wt("ac", dma_s, dma_s.n)

        def projection(w_t, colbase, swap_extra_waits=()):
            v_wsl, v_pmm, v_pevac, v_swpmm, v_swpevac, v_rot = {}, {}, {}, {}, {}, {}

            def swap_psum(m):
                return PS_B if (m % 2 == 0) else PS_C

            def emit_swap_mm(m):
                wt("pe", s_ac, v_pevac[m])
                if m <= 1:
                    for semx, vv in swap_extra_waits:
                        wt("pe", semx, vv)
                if m >= 2:
                    wt("pe", s_ac, v_swpevac[m - 2])
                sp = swap_psum(m)
                for hf in range(2):
                    op("pe", "matmul", sp.ap()[:, 512 * hf:512 * hf + 512],
                       SWAPM.ap(),
                       ap(bcA, colbase + 1024 * m + 512 * hf, [[32768, 128], [1, 512]]),
                       start=True, stop=True, inc=(s_pe if hf == 1 else None), **MM)
                v_swpmm[m] = s_pe.n

            def emit_swp_evac(m):
                wt("ac", s_pe, v_swpmm[m])
                if m >= 2:
                    wt("ac", s_dv, v_rot[m - 2])
                v_swpevac[m] = op("ac", "copy",
                                  QD.ap()[:, 1024 * (m % 2):1024 * (m % 2) + 1024],
                                  swap_psum(m).ap()[:, 0:1024], inc=s_ac)

            def emit_rot(m):
                wt("dv", s_ac, v_swpevac[m])
                tile = ap(bcA, colbase + 1024 * m, [[32768, 128], [1, 1024]])
                rott = ap(bcS, 0, [[8192, 128], [1, 1024]])
                rotu = ap(bcS, 1024, [[8192, 128], [1, 1024]])
                op("dv", "tensor_mul", rott, tile, COSR.ap())
                op("dv", "tensor_mul", rotu,
                   QD.ap()[:, 1024 * (m % 2):1024 * (m % 2) + 1024], SINR.ap())
                v_rot[m] = op("dv", "tensor_add", tile, rott, rotu, inc=s_dv)

            for m in range(16):
                if m >= 2:
                    wt("po", s_pe, v_pmm[m - 2])
                v_wsl[m] = dma("po", dma_p,
                               WSL.ap()[:, 2048 * (m % 2):2048 * (m % 2) + 2048],
                               ap(w_t, m * 128 * 2048, [[2048, 128], [1, 2048]]))
                wt("pe", dma_p, v_wsl[m])
                if m >= 2:
                    wt("pe", s_ac, v_pevac[m - 2])
                for k in range(KD):
                    for hf in range(2):
                        op("pe", "matmul",
                           PS_A.ap()[:, 1024 * (m % 2) + 512 * hf:
                                     1024 * (m % 2) + 512 * hf + 512],
                           WSL.ap()[:, 2048 * (m % 2) + 128 * k:
                                    2048 * (m % 2) + 128 * k + 128],
                           ap(bcC, 1024 * k + 512 * hf, [[16384, 128], [1, 512]]),
                           start=(k == 0), stop=(k == KD - 1),
                           inc=(s_pe if k == KD - 1 and hf == 1 else None), **MM)
                v_pmm[m] = s_pe.n
                if m >= 1:
                    emit_swap_mm(m - 1)
                wt("ac", s_pe, v_pmm[m])
                v_pevac[m] = op("ac", "copy",
                                ap(bcA, colbase + 1024 * m, [[32768, 128], [1, 1024]]),
                                PS_A.ap()[:, 1024 * (m % 2):1024 * (m % 2) + 1024],
                                inc=s_ac)
                if m >= 1:
                    emit_swp_evac(m - 1)
                    emit_rot(m - 1)
            emit_swap_mm(15)
            emit_swp_evac(15)
            emit_rot(15)
            return v_pmm, v_pevac, v_swpevac, v_rot

        vK_pmm, vK_pevac, vK_swpevac, vK_rot = projection(
            wk, KT0, swap_extra_waits=((s_ac, v_bcsb), (s_dv, v_lgtm)))

        def v_psum(c):
            if c < 4:
                return bass.AP(PS_A, 512 * c, [[2048, 128], [1, 512]])
            if c < 6:
                return PS_B.ap()[:, 512 * (c - 4):512 * (c - 4) + 512]
            return PS_C.ap()[:, 512 * (c - 6):512 * (c - 6) + 512]

        v_vp, v_vhalf, v_vcmm, v_vevac = {}, {}, {}, {}
        for g in range(4):
            for hf in range(2):
                if g >= 1:
                    wt("po", s_pe, v_vhalf[(g - 1, hf)])
                v_vp[(g, hf)] = dma("po", dma_p,
                                    ap(bcV, 4096 * hf, [[8192, 128], [1, 4096]]),
                                    ap(wv, g * 128 * 8192 + 4096 * hf,
                                       [[8192, 128], [1, 4096]]))
            for hf in range(2):
                wt("pe", dma_p, v_vp[(g, hf)])
                if g == 0 and hf == 0:
                    wt("pe", s_ac, vK_swpevac[15])
                if g >= 1 and hf == 0:
                    wt("pe", s_ac, v_vevac[(g - 1, 7)])
                for c in range(NCH):
                    for kk in range(8):
                        k = 8 * hf + kk
                        op("pe", "matmul", v_psum(c),
                           ap(bcC, 1024 * k + 128 * c, [[16384, 128], [1, 128]]),
                           ap(bcV, 4096 * hf + 512 * kk, [[8192, 128], [1, 512]]),
                           start=(k == 0), stop=(k == KD - 1),
                           inc=(s_pe if kk == 7 else None), **MM)
                    if hf == 1:
                        v_vcmm[(g, c)] = s_pe.n
                v_vhalf[(g, hf)] = s_pe.n
            for c in range(NCH):
                wt("ac", s_pe, v_vcmm[(g, c)])
                v_vevac[(g, c)] = op("ac", "copy",
                                     ap(bcB, 2048 * c + 512 * g, [[16384, 128], [1, 512]]),
                                     v_psum(c), inc=s_ac)
        barrier()
        if "v" in dbg:
            tap("v", ap(bcB, 0, [[16384, 128], [1, 16384]]))
        if "kT" in dbg:
            tap("kT", ap(bcA, KT0, [[32768, 128], [1, 16384]]))

        def kd_psum_ap(par, slot):
            return ap(bcPB, 1024 * par + 128 * slot, [[2048, 128], [1, 128]])

        v_kdT, v_kdevac = {}, {}
        v_p1state, v_p1stG, v_stbf1 = {}, {}, {}
        for c in range(NCH):
            if c > 0:
                wt("ac", s_pe, v_p1state[c - 1])
                v_stbf1[c] = op("ac", "copy", STBF.ap(), PS_A.ap()[:, 0:2048], inc=s_ac)

            def emit_kdT(G, c=c):
                gn = 4 * c + G
                if gn >= 2:
                    wt("pe", s_dv, v_kdevac[gn - 2])
                for h in range(4 * G, 4 * G + 4):
                    op("pe", "transpose", kd_psum_ap(gn % 2, h % 4),
                       ap(bcA, KT0 + 1024 * h + 128 * c, [[32768, 128], [1, 128]]),
                       IDENTB.ap(), inc=(s_pe if h % 4 == 3 else None))
                v_kdT[gn] = s_pe.n

            def emit_state(G, c=c):
                gn = 4 * c + G
                wt("pe", s_dv, v_kdevac[gn])
                if G == 0 and c > 0:
                    wt("pe", s_ac, v_stbf1[c])
                for h in range(4 * G, 4 * G + 4):
                    kd_sl = QD.ap()[:, 512 * (gn % 2) + 128 * (h % 4):
                                    512 * (gn % 2) + 128 * (h % 4) + 128]
                    v_sl = ap(bcB, 2048 * c + 128 * h, [[16384, 128], [1, 128]])
                    st_out = PS_A.ap()[:, 128 * h:128 * h + 128]
                    last = (s_pe if h % 4 == 3 else None)
                    if c == 0:
                        op("pe", "matmul", st_out, kd_sl, v_sl, start=True, stop=True,
                           inc=last, **MM)
                    else:
                        op("pe", "matmul", st_out, RCIC.ap()[:, 128 * h:128 * h + 128],
                           STBF.ap()[:, 128 * h:128 * h + 128], start=True, stop=False,
                           **MM)
                        op("pe", "matmul", st_out, kd_sl, v_sl, start=False, stop=True,
                           inc=last, **MM)
                v_p1stG[gn] = s_pe.n
                if G == 3:
                    v_p1state[c] = s_pe.n

            def emit_kdevac(G, c=c):
                gn = 4 * c + G
                wt("dv", s_pe, v_kdT[gn])
                if gn >= 2:
                    wt("dv", s_pe, v_p1stG[gn - 2])
                for h in range(4 * G, 4 * G + 4):
                    op("dv", "tensor_scalar_mul",
                       QD.ap()[:, 512 * (gn % 2) + 128 * (h % 4):
                               512 * (gn % 2) + 128 * (h % 4) + 128],
                       kd_psum_ap(gn % 2, h % 4), KDECC.ap()[:, h:h + 1],
                       inc=(s_dv if h % 4 == 3 else None))
                v_kdevac[gn] = s_dv.n

            emit_kdT(0)
            emit_kdevac(0)
            emit_kdT(1)
            emit_state(0)
            emit_kdevac(1)
            emit_kdT(2)
            emit_state(1)
            emit_kdevac(2)
            emit_kdT(3)
            emit_state(2)
            emit_kdevac(3)
            emit_state(3)

        v_kdt2, v_kdp = {}, {}

        def kd_home(c, h):
            if c < 2:
                return ap(bcV, 4096 + 2048 * c + 128 * h, [[8192, 128], [1, 128]])
            return ap(bcC, 2048 * c + 128 * h, [[16384, 128], [1, 128]])

        def emit_kd_dma(c):
            if c == 2:
                wt("sy", s_pe, v_hmmall)
            vv = 0
            for h in range(H):
                vv = dma("sy", dma_s, kd_home(c, h),
                         ap(bcA, KT0 + 1024 * h + 128 * c, [[32768, 128], [1, 128]]),
                         transpose=True)
            v_kdt2[c] = vv

        def emit_kdp(c):
            wt("dv", dma_s, v_kdt2[c])
            for h in range(H):
                sl = kd_home(c, h)
                op("dv", "tensor_scalar_mul", sl, sl, KDECC.ap()[:, h:h + 1],
                   inc=(s_dv if h == H - 1 else None))
            v_kdp[c] = s_dv.n

        wt("ac", s_pe, v_p1state[NCH - 1])
        v_Sevac = op("ac", "copy", STBF.ap(), PS_A.ap()[:, 0:2048], inc=s_ac)
        wt("sy", s_ac, v_Sevac)
        v_ccin = dma("sy", dma_s, cc_in.ap(), STBF.ap())
        if "S" in dbg:
            tap("S", STBF.ap())
        emit_kd_dma(0)
        emit_kd_dma(1)
        wt("po", dma_s, v_ccin)
        v_ccc = s_cc.bump()
        em("po", lambda e, s=s_cc.h: e.collective_compute(
            "AllGather", ALU.bypass, replica_groups=[list(range(NCORES))],
            ins=[cc_in.ap().opt()], outs=[cc_out.ap().opt()]).then_inc(s, 1))

        wt("po", s_pe, vK_pmm[15])
        wt("pe", s_ac, v_Sevac)
        wt("pe", s_dv, v_kdevac[4 * NCH - 1])
        vQ_pmm, vQ_pevac, vQ_swpevac, vQ_rot = projection(wq, 0)

        def h_psum(c):
            if c < 4:
                return bass.AP(PS_A, 512 * c, [[2048, 128], [1, 512]])
            if c < 6:
                return PS_B.ap()[:, 512 * (c - 4):512 * (c - 4) + 512]
            return PS_C.ap()[:, 512 * (c - 6):512 * (c - 6) + 512]

        v_asl, v_hkmm = {}, {}
        for k in range(KD):
            if k >= 2:
                wt("po", s_pe, v_hkmm.get(k - 2, 0))
            v_asl[k] = dma("po", dma_p,
                           BWS.ap()[:, 512 * (k % 2):512 * (k % 2) + 512],
                           ap(aall, 512 * k, [[8192, 128], [1, 512]]))
            wt("pe", dma_p, v_asl[k])
            if k == 0:
                wt("pe", s_ac, vQ_swpevac[15])
            for c in range(NCH):
                op("pe", "matmul", h_psum(c),
                   ap(bcC, 1024 * k + 128 * c, [[16384, 128], [1, 128]]),
                   BWS.ap()[:, 512 * (k % 2):512 * (k % 2) + 512],
                   start=(k == 0), stop=(k == KD - 1),
                   inc=(s_pe if c == NCH - 1 else None), **MM)
            v_hkmm[k] = s_pe.n
        v_hmmall = v_hkmm[KD - 1]

        def hsc_slot(c):
            if c < 4:
                return STBF.ap()[:, 512 * c:512 * c + 512]
            if c < 6:
                return SCM.ap()[:, 512 * (c - 4):512 * (c - 4) + 512]
            return ap(bcS, 512 * (c - 6), [[8192, 128], [1, 512]])

        HORD = [4, 5, 6, 7, 0, 1, 2, 3]
        v_hevac2 = {}
        wt("dv", s_pe, v_hmmall)
        for c in HORD:
            for ee in range(8):
                hp = h_psum(c)
                op("dv", "tensor_scalar_mul",
                   bass.AP(hsc_slot(c).tensor, hsc_slot(c).offset + 64 * ee,
                           [[hsc_slot(c).tensor.shape[-1], 128], [1, 64]]),
                   bass.AP(hp.tensor, hp.offset + 64 * ee,
                           [[hp.tensor.shape[-1], 128], [1, 64]]),
                   GW.ap()[:, 432 + 8 * c + ee:433 + 8 * c + ee],
                   inc=(s_dv if ee == 7 else None))
            v_hevac2[c] = s_dv.n

        v_wslw, v_bwsw = {}, {}
        wt("po", s_pe, v_hmmall)
        v_wslw[0] = dma("po", dma_p, WSL.ap()[:, 0:2048],
                        ap(wo, 0, [[2048, 128], [1, 2048]]))
        v_bwsw[0] = dma("po", dma_p, BWS.ap()[:, 0:512],
                        ap(bwm, 0, [[512, 128], [1, 512]]))
        wt("po", s_pe, vQ_pmm[15])
        v_wslw[1] = dma("po", dma_p, WSL.ap()[:, 2048:4096],
                        ap(wo, 128 * 2048, [[2048, 128], [1, 2048]]))
        v_bwsw[1] = dma("po", dma_p, BWS.ap()[:, 512:1024],
                        ap(bwm, 128 * 512, [[512, 128], [1, 512]]))

        emit_kdp(0)
        emit_kdp(1)
        kvin_bf = ap(bcS, 4096, [[8192, 128], [1, 2048]])
        v_gth, v_comb = {}, {}
        wt("sy", s_cc, v_ccc)
        for i in range(NCORES):
            if i >= 2:
                wt("sy", s_dv, v_comb[i - 2])
            v_gth[i] = dma("sy", dma_s,
                           ap(bcV, 2048 * (i % 2), [[8192, 128], [1, 2048]]),
                           ap(cc_out, i * 128 * 2048, [[2048, 128], [1, 2048]]))
            wt("dv", dma_s, v_gth[i])
            for h in range(H):
                gsl = ap(bcV, 2048 * (i % 2) + 128 * h, [[8192, 128], [1, 128]])
                ksl = ap(bcS, 4096 + 128 * h, [[8192, 128], [1, 128]])
                co = COEFC.ap()[:, 16 * i + h:16 * i + h + 1]
                last = (s_dv if h == H - 1 else None)
                if i == 0:
                    op("dv", "tensor_scalar_mul", ksl, gsl, co, inc=last)
                else:
                    op("dv", "scalar_tensor_tensor", ksl, gsl, co, ksl,
                       ALU.mult, ALU.add, inc=last)
            v_comb[i] = s_dv.n
        if "KVin" in dbg:
            wt("sy", s_dv, v_comb[NCORES - 1])
            tap("KVin", kvin_bf)


        v_hT = {}
        for c in HORD:
            wt("ac", s_dv, v_hevac2[c])
            for b in range(4):
                sl = hsc_slot(c)
                v_hT[c] = dma("ac", dma_a,
                              HT.ap()[:, 1024 * b + 128 * c:1024 * b + 128 * c + 128],
                              bass.AP(sl.tensor, sl.offset + 128 * b,
                                      [[sl.tensor.shape[-1], 128], [1, 128]]),
                              transpose=True)
        wt("pe", s_dv, v_comb[NCORES - 1])

        attn_sc = SCR.ap()[:, 0:2048]
        v_qd, v_stbf2, v_scores, v_maskevac, v_outmm = {}, {}, {}, {}, {}
        v_outevac, v_p2state, v_sqg, v_sqmm, v_rmsrow = {}, {}, {}, {}, {}
        v_rstdbcmm, v_rstdsb, v_rmsg = {}, {}, {}
        for c in range(NCH):
            if c + 2 < NCH:
                emit_kd_dma(c + 2)
            if c > 0:
                wt("dv", s_pe, v_p2state[c - 1])
            v_qd[c] = op("dv", "tensor_mul",
                         ap(QD, 0, [[2048, 128], [128, 16], [1, 128]]),
                         ap(bcA, 128 * c, [[32768, 128], [1024, 16], [1, 128]]),
                         ap(QDECC, 0, [[2048, 128], [128, 16], [1, 128]]),
                         inc=s_dv)
            if c > 0:
                if c == 1:
                    wt("ac", dma_a, v_hT[3])
                wt("ac", s_pe, v_p2state[c - 1])
                v_stbf2[c] = op("ac", "copy", STBF.ap(), PS_A.ap()[:, 0:2048], inc=s_ac)

            def emit_scores(G, c=c):
                gn = 4 * c + G
                if gn >= 2:
                    wt("pe", s_dv, v_maskevac[gn - 2])
                for h in range(4 * G, 4 * G + 4):
                    sl = PS_B.ap()[:, 512 * (gn % 2) + 128 * (h % 4):
                                   512 * (gn % 2) + 128 * (h % 4) + 128]
                    op("pe", "matmul", sl,
                       ap(bcA, KT0 + 1024 * h + 128 * c, [[32768, 128], [1, 128]]),
                       ap(bcA, 1024 * h + 128 * c, [[32768, 128], [1, 128]]),
                       start=True, stop=True, inc=(s_pe if h % 4 == 3 else None), **MM)
                v_scores[gn] = s_pe.n

            def emit_mask(G, c=c):
                gn = 4 * c + G
                if c == 0 and G == 0:
                    wt("dv", dma_a, v_hT[5])
                wt("dv", s_pe, v_scores[gn])
                prevkey = (c, G - 2) if G >= 2 else (c - 1, G + 2)
                if prevkey in v_outmm:
                    wt("dv", s_pe, v_outmm[prevkey])
                op("dv", "tensor_mul", SCM.ap()[:, 512 * (gn % 2):512 * (gn % 2) + 512],
                   PS_B.ap()[:, 512 * (gn % 2):512 * (gn % 2) + 512],
                   DMASKC.ap()[:, 512 * G:512 * G + 512], inc=s_dv)
                v_maskevac[gn] = s_dv.n

            def emit_out(G, c=c):
                gn = 4 * c + G
                wt("pe", s_dv, v_maskevac[gn])
                if G == 0:
                    wt("pe", s_dv, v_qd[c])
                    if c > 0:
                        wt("pe", s_ac, v_stbf2[c])
                        wt("pe", s_ac, v_rmsrow[c - 1])
                    else:
                        wt("pe", s_dv, v_comb[NCORES - 1])
                if G == 1 and c > 0:
                    wt("pe", s_ac, v_rstdsb[c - 1])
                prevkey = (c, G - 2) if G >= 2 else (c - 1, G + 2)
                if prevkey in v_outevac:
                    wt("pe", s_ac, v_outevac[prevkey])
                for h in range(4 * G, 4 * G + 4):
                    sl = PS_C.ap()[:, 512 * (gn % 2) + 128 * (h % 4):
                                   512 * (gn % 2) + 128 * (h % 4) + 128]
                    op("pe", "matmul", sl,
                       ap(bcB, 2048 * c + 128 * h, [[16384, 128], [1, 128]]),
                       SCM.ap()[:, 512 * (gn % 2) + 128 * (h % 4):
                                512 * (gn % 2) + 128 * (h % 4) + 128],
                       start=True, stop=False, **MM)
                    if c == 0:
                        stsl = ap(bcS, 4096 + 128 * h, [[8192, 128], [1, 128]])
                    else:
                        stsl = STBF.ap()[:, 128 * h:128 * h + 128]
                    op("pe", "matmul", sl, stsl, QD.ap()[:, 128 * h:128 * h + 128],
                       start=False, stop=True, inc=(s_pe if h % 4 == 3 else None), **MM)
                v_outmm[(c, G)] = s_pe.n

            def emit_state2(G, c=c):
                if c == NCH - 1:
                    return
                if G == 0:
                    wt("pe", s_dv, v_kdp[c])
                for h in range(4 * G, 4 * G + 4):
                    st_out = PS_A.ap()[:, 128 * h:128 * h + 128]
                    if c == 0:
                        ssl = ap(bcS, 4096 + 128 * h, [[8192, 128], [1, 128]])
                    else:
                        ssl = STBF.ap()[:, 128 * h:128 * h + 128]
                    op("pe", "matmul", st_out, RCIC.ap()[:, 128 * h:128 * h + 128],
                       ssl, start=True, stop=False, **MM)
                    op("pe", "matmul", st_out, kd_home(c, h),
                       ap(bcB, 2048 * c + 128 * h, [[16384, 128], [1, 128]]),
                       start=False, stop=True,
                       inc=(s_pe if G == 3 and h % 4 == 3 else None), **MM)
                if G == 3:
                    v_p2state[c] = s_pe.n

            def emit_outevac(G, c=c):
                gn = 4 * c + G
                if c == 0 and G == 0:
                    wt("ac", dma_a, v_hT[7])
                wt("ac", s_pe, v_outmm[(c, G)])
                if c > 0:
                    wt("ac", s_dv, v_rmsg[(c - 1, G)])
                v_outevac[(c, G)] = op("ac", "copy",
                                       SCR.ap()[:, 512 * G:512 * G + 512],
                                       PS_C.ap()[:, 512 * (gn % 2):512 * (gn % 2) + 512],
                                       inc=s_ac)
                v_sqg[(c, G)] = op("ac", "activation",
                                   ap(bcS, 6144 + 512 * G, [[8192, 128], [1, 512]]),
                                   SCR.ap()[:, 512 * G:512 * G + 512], AF.Square,
                                   inc=s_ac)

            def emit_sqmm(G, c=c):
                if G == 0:
                    wt("pe", s_ac, v_sqg[(c, 2)])
                wt("pe", s_ac, v_sqg[(c, G)])
                for h in range(4 * G, 4 * G + 4):
                    op("pe", "matmul", PS_C.ap()[0:1, 0:128], ONESC.ap(),
                       ap(bcS, 6144 + 128 * h, [[8192, 128], [1, 128]]),
                       start=(h == 0), stop=(h == H - 1),
                       inc=(s_pe if h == H - 1 else None), **MM)
                if G == 3:
                    v_sqmm[c] = s_pe.n

            def emit_rms(c2, Gs, c=c):
                if Gs[0] == 0:
                    wt("dv", s_ac, v_rstdsb[c2])
                    wt("dv", s_pe,
                       v_p2state[c2] if c2 < NCH - 1 else v_outmm[(c2, 3)])
                for G in Gs:
                    op("dv", "tensor_mul",
                       ap(SCR, 512 * G, [[4096, 128], [128, 4], [1, 128]]),
                       ap(SCR, 512 * G, [[4096, 128], [128, 4], [1, 128]]),
                       ap(VPX, 0, [[4096, 128], [0, 4], [1, 128]]))
                    v_rmsg[(c2, G)] = op("dv", "tensor_mul",
                                         ap(bcB, 2048 * c2 + 512 * G,
                                            [[16384, 128], [1, 128], [128, 4]]),
                                         ap(SCR, 512 * G,
                                            [[4096, 128], [1, 128], [128, 4]]),
                                         ap(RMSC, 4 * G, [[16, 128], [0, 128], [1, 4]]),
                                         inc=s_dv)

            emit_scores(0)
            emit_mask(0)
            emit_scores(1)
            emit_out(0)
            emit_mask(1)
            if c >= 1:
                emit_rms(c - 1, [0, 1])
            emit_outevac(0)
            emit_scores(2)
            emit_out(1)
            emit_state2(0)
            emit_mask(2)
            emit_outevac(1)
            emit_scores(3)
            emit_out(2)
            emit_state2(1)
            emit_mask(3)
            if c >= 1:
                emit_rms(c - 1, [2, 3])
            if c + 2 < NCH:
                emit_kdp(c + 2)
            emit_outevac(2)
            emit_out(3)
            emit_state2(2)
            emit_sqmm(0)
            emit_sqmm(1)
            emit_sqmm(2)
            emit_state2(3)
            emit_outevac(3)
            emit_sqmm(3)
            wt("ac", s_pe, v_sqmm[c])
            op("ac", "activation", GW.ap()[0:1, 16:144], PS_C.ap()[0:1, 0:128],
               AF.Ln, bias=EPSC.ap()[0:1, :], scale=1.0 / D)
            v_rmsrow[c] = op("ac", "activation", GW.ap()[0:1, 144:272],
                             GW.ap()[0:1, 16:144], AF.Exp, scale=-0.5, inc=s_ac)
            wt("pe", s_ac, v_rmsrow[c])
            op("pe", "matmul", PS_C.ap()[:, 512:640], GW.ap()[0:1, 304:432],
               GW.ap()[0:1, 144:272], start=True, stop=True, inc=s_pe, **MM)
            v_rstdbcmm[c] = s_pe.n
            wt("ac", s_pe, v_rstdbcmm[c])
            v_rstdsb[c] = op("ac", "copy", VPX.ap()[:, 0:128], PS_C.ap()[:, 512:640],
                             inc=s_ac)
            last_rms = emit_rms
        last_rms(NCH - 1, [0, 1, 2, 3])
        barrier()
        if "attn" in dbg:
            tap("attn", ap(bcB, 0, [[16384, 128], [1, 16384]]))
        if "kd" in dbg:
            tap("kd", ap(bcC, 0, [[16384, 128], [1, 16384]]))

        v_wmm, v_wevac, v_xsc, v_outst = {}, {}, {}, {}
        for m in range(16):
            if m >= 2:
                wt("po", s_pe, v_wmm[m - 2])
                v_wslw[m] = dma("po", dma_p,
                                WSL.ap()[:, 2048 * (m % 2):2048 * (m % 2) + 2048],
                                ap(wo, m * 128 * 2048, [[2048, 128], [1, 2048]]))
                v_bwsw[m] = dma("po", dma_p,
                                BWS.ap()[:, 512 * (m % 2):512 * (m % 2) + 512],
                                ap(bwm, m * 128 * 512, [[512, 128], [1, 512]]))
            if m >= 2:
                wt("sy", s_dv, v_wevac[m - 2])
            v_xsc[m] = dma("sy", dma_s,
                           VPX.ap()[:, 1024 * (m % 2):1024 * (m % 2) + 1024],
                           ap(xt, m * 128 * 1024, [[1024, 128], [1, 1024]]))
            wt("pe", dma_p, v_wslw[m])
            wt("pe", dma_p, v_bwsw[m])
            if m >= 2:
                wt("pe", s_dv, v_wevac[m - 2])
            for hf in range(2):
                pout = PS_A.ap()[:, 1024 * (m % 2) + 512 * hf:
                                 1024 * (m % 2) + 512 * hf + 512]
                for k in range(KD):
                    op("pe", "matmul", pout,
                       WSL.ap()[:, 2048 * (m % 2) + 128 * k:
                                2048 * (m % 2) + 128 * k + 128],
                       ap(bcB, 8192 * hf + 128 * k, [[16384, 128], [2048, 4], [1, 128]]),
                       start=(k == 0), stop=False, **MM)
                for k4 in range(4):
                    op("pe", "matmul", pout,
                       BWS.ap()[:, 512 * (m % 2) + 128 * k4:
                                512 * (m % 2) + 128 * k4 + 128],
                       HT.ap()[:, 1024 * k4 + 512 * hf:1024 * k4 + 512 * hf + 512],
                       start=False, stop=(k4 == 3),
                       inc=(s_pe if hf == 1 and k4 == 3 else None), **MM)
            v_wmm[m] = s_pe.n
            wt("dv", s_pe, v_wmm[m])
            wt("dv", dma_s, v_xsc[m])
            if m >= 2:
                wt("dv", dma_s, v_outst[m - 2])
            v_wevac[m] = op("dv", "tensor_add",
                            VPX.ap()[:, 2048 + 1024 * (m % 2):2048 + 1024 * (m % 2) + 1024],
                            PS_A.ap()[:, 1024 * (m % 2):1024 * (m % 2) + 1024],
                            VPX.ap()[:, 1024 * (m % 2):1024 * (m % 2) + 1024],
                            inc=s_dv)
            wt("sy", s_dv, v_wevac[m])
            v_outst[m] = dma("sy", dma_s,
                             ap(out_t, m * 128 * 1024, [[1024, 128], [1, 1024]]),
                             VPX.ap()[:, 2048 + 1024 * (m % 2):
                                      2048 + 1024 * (m % 2) + 1024])
        wt("sy", dma_s, v_outst[15])

        block = es.enter_context(nc.Block())

        @block.tensor
        def _(e):
            for f in ST["pe"]:
                f(e)

        @block.vector
        def _(e):
            for f in ST["dv"]:
                f(e)

        @block.scalar
        def _(e):
            for f in ST["ac"]:
                f(e)

        @block.gpsimd
        def _(e):
            for f in ST["po"]:
                f(e)

        @block.sync
        def _(e):
            for f in ST["sy"]:
                f(e)

    return nc


_CACHE = {}


def _get_nc():
    if "nc" not in _CACHE:
        _CACHE["nc"] = build_nc(())
    return _CACHE["nc"]


def kernel(L0_attn_x, L0_attn_Wq, L0_attn_Wk, L0_attn_Wv, L0_attn_Wo,
           L0_attn_rms_w, L0_ffn_A, L0_ffn_Bw, L0_ffn_Wg, _taps=None):
    x = np.asarray(L0_attn_x)
    wts = _host_weights(np.asarray(L0_attn_Wq), np.asarray(L0_attn_Wk),
                        np.asarray(L0_attn_Wv), np.asarray(L0_attn_Wo),
                        np.asarray(L0_attn_rms_w), np.asarray(L0_ffn_A),
                        np.asarray(L0_ffn_Bw), np.asarray(L0_ffn_Wg))
    consts = _host_constants()
    nc = build_nc(_taps) if _taps else _get_nc()
    in_maps = []
    for j in range(NCORES):
        m = {"xt": np.ascontiguousarray(x[0, j * NTOK:(j + 1) * NTOK, :].T)}
        m.update(wts)
        m.update(consts)
        m.update(_host_core_constants(j))
        in_maps.append(m)
    res = run_bass_kernel_spmd(nc, in_maps, core_ids=list(range(NCORES)))
    out = np.concatenate([res.results[j]["out"].T for j in range(NCORES)], axis=0)
    out = out.reshape(1, 8192, 2048).astype(np.float32)
    if _taps:
        return out, res
    return out
